# version 16
# baseline (speedup 1.0000x reference)
import numpy as np
import ml_dtypes
import contextlib
import concourse.bass as bass
import concourse.mybir as mybir
from concourse.bass_utils import run_bass_kernel_spmd
from concourse.alu_op_type import AluOpType as OP

F32 = mybir.dt.float32
BF16 = mybir.dt.bfloat16
F16 = mybir.dt.float16
AF = mybir.ActivationFunctionType

D = 2048
NCH = 16
NT = 1024
NH = 16
HD = 128
II = 8192
SCALE = 0.08838834764831845
CAP = 30.0
LN_EPS = 1e-5
QKV_OUT = 2560

IBW = 256
PBW = 512
SPW = 2048
NSP = II // SPW
NPB = SPW // PBW
NIB = PBW // IBW

PHASES = ("ffn1", "attn", "ffn2")
ATTN_PARTS = ("k0", "k1", "v0", "v1", "send", "cc", "q", "heads", "wo", "rot")

ENGS = ("tensor", "vector", "scalar", "gpsimd", "sync")


class Rec:

    def __init__(self, sems):
        self.ops = {e: [] for e in ENGS}
        self.sems = sems
        self.cnt = {k: 0 for k in sems}
        self.last_wait = {}

    def wait(self, eng, key, val):
        if val is None or val <= 0:
            return
        if self.last_wait.get((eng, key), 0) >= val:
            return
        self.last_wait[(eng, key)] = val
        self.ops[eng].append(("w", key, val))

    def wait_tok(self, eng, tok):
        if tok is not None:
            self.wait(eng, tok[0], tok[1])

    def emit(self, eng, fn, inc=None, n=1):
        self.ops[eng].append(("c", fn, inc, n))
        if inc is not None:
            self.cnt[inc] += n
            return (inc, self.cnt[inc])
        return None

    def tok(self, key):
        return (key, self.cnt[key])


def build_program():
    nc = bass.Bass("TRN2", target_bir_lowering=False)

    x_in = nc.dram_tensor("x", [NT, D], F32, kind="ExternalInput").ap()
    wqkv = nc.dram_tensor("wqkv", [D, QKV_OUT], F32, kind="ExternalInput").ap()
    wo = nc.dram_tensor("wo", [D, D], F32, kind="ExternalInput").ap()
    wg = nc.dram_tensor("wg", [D, II], F32, kind="ExternalInput").ap()
    wu = nc.dram_tensor("wu", [D, II], F32, kind="ExternalInput").ap()
    wd = nc.dram_tensor("wd", [II, D], F32, kind="ExternalInput").ap()
    tbl_a_in = nc.dram_tensor("tbl_a", [128, NT], F16, kind="ExternalInput").ap()
    tbl_b_in = nc.dram_tensor("tbl_b", [128, NT], F16, kind="ExternalInput").ap()
    mask_p0_in = nc.dram_tensor("mask_p0", [128, 128], BF16, kind="ExternalInput").ap()
    mask_p1_in = nc.dram_tensor("mask_p1", [128, 128], BF16, kind="ExternalInput").ap()
    ident32_in = nc.dram_tensor("ident32", [128, 128], F32, kind="ExternalInput").ap()
    ident16_in = nc.dram_tensor("ident16", [128, 128], F16, kind="ExternalInput").ap()
    ones_in = nc.dram_tensor("ones_bf", [128, 128], BF16, kind="ExternalInput").ap()
    eps_in = nc.dram_tensor("eps32", [128, 1], F32, kind="ExternalInput").ap()
    out = nc.dram_tensor("out", [NT, D], F32, kind="ExternalOutput").ap()
    cc_send = nc.dram_tensor("cc_send", [512, NT], BF16)
    cc_recv = nc.dram_tensor("cc_recv", [1024, NT], BF16)

    st = contextlib.ExitStack()
    with st:
        X = st.enter_context(nc.sbuf_tensor("Xst", [128, NCH * NT], F32))
        XN = st.enter_context(nc.sbuf_tensor("XNst", [128, NCH * NT], BF16))
        HSP = st.enter_context(nc.sbuf_tensor("HSPst", [128, 16 * NT], BF16))
        H = st.enter_context(nc.sbuf_tensor("Hst", [128, 8 * PBW], F16))
        WGU = st.enter_context(nc.sbuf_tensor("WGUst", [128, 2 * NCH * IBW], BF16))
        WCOL = st.enter_context(nc.sbuf_tensor("WCOLst", [128, 2 * 2048], BF16))
        SU = st.enter_context(nc.sbuf_tensor("SUst", [128, 8 * IBW], F16))
        SPA = st.enter_context(nc.sbuf_tensor("SPAst", [128, 2 * PBW], F16))
        SPM = st.enter_context(nc.sbuf_tensor("SPMst", [128, 4 * 256], F16))
        MSK = st.enter_context(nc.sbuf_tensor("MSKst", [128, 256], F16))
        ESB = st.enter_context(nc.sbuf_tensor("ESBst", [128, 1024], F16))
        EBF = st.enter_context(nc.sbuf_tensor("EBFst", [128, 2 * 1024], BF16))
        KG = st.enter_context(nc.sbuf_tensor("KGst", [128, 2048], BF16))
        VT = st.enter_context(nc.sbuf_tensor("VTst", [128, 2048], BF16))
        RC = st.enter_context(nc.sbuf_tensor("RCst", [128, 512], F32))
        TBA = st.enter_context(nc.sbuf_tensor("TBAst", [128, NT], F16))
        TBB = st.enter_context(nc.sbuf_tensor("TBBst", [128, NT], F16))
        MU = st.enter_context(nc.sbuf_tensor("MUst", [128, 512], BF16))
        RS = st.enter_context(nc.sbuf_tensor("RSst", [128, 512], F32))
        SCF = st.enter_context(nc.sbuf_tensor("SCFst", [128, 2 * 512], F16))
        SCB = st.enter_context(nc.sbuf_tensor("SCBst", [128, 2 * 1024], BF16))
        QB = st.enter_context(nc.sbuf_tensor("QBst", [128, 2 * 1024], BF16))
        XSTG = st.enter_context(nc.sbuf_tensor("XSTGst", [128, 1024], F32))
        ID32 = st.enter_context(nc.sbuf_tensor("ID32st", [128, 128], F32))
        ID16 = st.enter_context(nc.sbuf_tensor("ID16st", [128, 128], F16))
        ONES = st.enter_context(nc.sbuf_tensor("ONESst", [128, 128], BF16))
        MP0 = st.enter_context(nc.sbuf_tensor("MP0st", [128, 128], BF16))
        MP1 = st.enter_context(nc.sbuf_tensor("MP1st", [128, 128], BF16))
        EPS = st.enter_context(nc.sbuf_tensor("EPSst", [128, 1], F32))

        PBIG = st.enter_context(nc.psum_tensor("PBIG", [128, 2048], F32))
        PC0 = st.enter_context(nc.psum_tensor("PC0", [128, 512], F32))
        PC1 = st.enter_context(nc.psum_tensor("PC1", [128, 512], F32))
        pt = {}

        keys = ["sW0", "sW1", "sC0", "sC1", "sX0", "sX1", "sIO", "sCC",
                "sPE", "sACT", "sDVE"]
        sems = {}
        for k in keys:
            sems[k] = st.enter_context(nc.semaphore(k))
        blk = st.enter_context(nc.Block())

        r = Rec(sems)
        pfree = {}

        def pwait(eng, *regions):
            for reg in regions:
                r.wait_tok(eng, pfree.get(reg))

        def Xv(c, a=0, b=NT):
            return X[:, c * NT + a: c * NT + b]

        def XNv(c, a=0, b=NT):
            return XN[:, c * NT + a: c * NT + b]

        def HSPv(c, a=0, b=NT):
            return HSP[:, c * NT + a: c * NT + b]

        def big(q):
            return PBIG[:, q * 512:(q + 1) * 512]

        def bighalf(j):
            return PBIG[:, j * 1024:(j + 1) * 1024]

        for dst, src in ((TBA, tbl_a_in), (TBB, tbl_b_in), (MP0, mask_p0_in),
                         (MP1, mask_p1_in), (ID32, ident32_in), (ID16, ident16_in),
                         (ONES, ones_in), (EPS, eps_in)):
            r.emit("sync", lambda h, d=dst, s=src: h.dma_start(out=d[:, :], in_=s),
                   inc="sIO", n=16)
        consts_tok = r.tok("sIO")
        for e in ("tensor", "vector", "scalar"):
            r.wait_tok(e, consts_tok)

        xin_evac = {}
        xstg_free = {0: None, 1: None}
        nx = {0: 0, 1: 0}
        for ts in range(8):
            for dh in range(2):
                for k4 in range(2):
                    half = (ts * 4 + dh * 2 + k4) % 2
                    skey = "sX%d" % half
                    qeng = "sync" if half == 0 else "scalar"
                    stg = XSTG[:, half * 512:(half + 1) * 512]
                    r.wait_tok(qeng, xstg_free[half])
                    nx[half] += 1
                    r.emit(qeng, lambda h, ts=ts, dh=dh, k4=k4, stg=stg:
                           h.dma_start(out=stg,
                                       in_=x_in[ts * 128:(ts + 1) * 128,
                                                dh * 1024 + k4 * 512:
                                                dh * 1024 + (k4 + 1) * 512]),
                           inc=skey, n=16)
                    r.wait(eng="tensor", key=skey, val=nx[half] * 16)
                    q = k4
                    pwait("tensor", ("big", q))
                    t_pe = None
                    for k in range(4):
                        t_pe = r.emit("tensor", lambda h, q=q, k=k, stg=stg:
                                      h.transpose(big(q)[:, k * 128:(k + 1) * 128],
                                                  stg[:, k * 128:(k + 1) * 128],
                                                  ID32[:, :]),
                                      inc="sPE" if k == 3 else None)
                    xstg_free[half] = r.tok("sPE")
                    r.wait_tok("scalar", t_pe)
                    c0 = dh * 8 + k4 * 4
                    dstv = X[:].rearrange("p (c t) -> p c t", c=NCH)[
                        :, c0:c0 + 4, ts * 128:(ts + 1) * 128]
                    srcv = big(q).rearrange("p (k t) -> p k t", k=4)
                    ev = r.emit("scalar", lambda h, dstv=dstv, srcv=srcv:
                                h.copy(dstv, srcv), inc="sACT")
                    pfree[("big", q)] = ev
                    for c in range(c0, c0 + 4):
                        xin_evac[c] = ev

        x_ready = dict(xin_evac)

        wgu_slot_tok = {0: None, 1: None}
        wgu_loads = {0: 0, 1: 0}
        wcol_slot_tok = {0: None, 1: None}
        wcol_loads = {0: 0, 1: 0}

        def load_wgu(src_ap, slot):
            skey = "sW%d" % slot
            r.wait_tok("gpsimd", wgu_slot_tok[slot])
            dst = WGU[:, slot * NCH * IBW:(slot + 1) * NCH * IBW].rearrange(
                "p (c i) -> p c i", c=NCH)
            src = src_ap.rearrange("(c p) i -> p c i", p=128)
            r.emit("gpsimd", lambda h, dst=dst, src=src: h.dma_start(out=dst, in_=src),
                   inc=skey, n=16)
            wgu_loads[slot] += 1
            return (skey, wgu_loads[slot] * 16)

        def load_wcol(src_ap, slot):
            skey = "sC%d" % slot
            r.wait_tok("gpsimd", wcol_slot_tok[slot])
            dst = WCOL[:, slot * 2048:(slot + 1) * 2048].rearrange(
                "p (c i) -> p c i", c=NCH)
            src = src_ap.rearrange("(c p) i -> p c i", p=128)
            r.emit("gpsimd", lambda h, dst=dst, src=src: h.dma_start(out=dst, in_=src),
                   inc=skey, n=16)
            wcol_loads[slot] += 1
            return (skey, wcol_loads[slot] * 16)

        def layernorm(x_tok_map):
            for q in range(4):
                pwait("tensor", ("big", q))
            stats_tok = {}
            xn_half = {}
            for c in range(NCH):
                buf = SCB if c % 2 == 0 else QB
                r.wait_tok("scalar", x_tok_map.get(c))
                if c >= 2:
                    r.wait_tok("scalar", stats_tok[c - 2])
                r.emit("scalar", lambda h, c=c, buf=buf: h.copy(buf[:, 0:1024], Xv(c)))
                ct = r.emit("scalar", lambda h, c=c, buf=buf:
                            h.activation(out=buf[:, 1024:2048], in_=Xv(c),
                                         func=AF.Square), inc="sACT")
                r.wait_tok("tensor", ct)
                for th in range(2):
                    r.emit("tensor", lambda h, c=c, th=th, buf=buf:
                           h.matmul(big(th), ONES[:, :],
                                    buf[:, th * 512:(th + 1) * 512],
                                    start=(c == 0), stop=(c == NCH - 1)))
                    r.emit("tensor", lambda h, c=c, th=th, buf=buf:
                           h.matmul(big(2 + th), ONES[:, :],
                                    buf[:, 1024 + th * 512: 1024 + (th + 1) * 512],
                                    start=(c == 0), stop=(c == NCH - 1)),
                           inc="sPE" if th == 1 else None)
                stats_tok[c] = r.tok("sPE")
            for th in range(2):
                r.wait_tok("scalar", stats_tok[NCH - 1])
                r.emit("scalar", lambda h, th=th:
                       h.activation(out=MU[:, :], in_=big(th), func=AF.Copy,
                                    scale=1.0 / D))
                e2t = r.emit("scalar", lambda h, th=th:
                             h.activation(out=XSTG[:, 0:512], in_=big(2 + th),
                                          func=AF.Copy, scale=1.0 / D), inc="sACT")
                r.wait_tok("vector", e2t)
                r.emit("vector", lambda h: h.tensor_tensor(
                    XSTG[:, 512:1024], MU[:, :], MU[:, :], OP.mult))
                vt = r.emit("vector", lambda h:
                            h.tensor_tensor(XSTG[:, 0:512], XSTG[:, 0:512],
                                            XSTG[:, 512:1024], OP.subtract), inc="sDVE")
                r.wait_tok("scalar", vt)
                r.emit("scalar", lambda h:
                       h.activation(out=XSTG[:, 512:1024], in_=XSTG[:, 0:512],
                                    func=AF.Ln, bias=EPS[:, 0:1]))
                sdt = r.emit("scalar", lambda h:
                             h.activation(out=RS[:, :], in_=XSTG[:, 512:1024],
                                          func=AF.Exp, scale=-0.5), inc="sACT")
                r.wait_tok("vector", sdt)
                for c in range(NCH):
                    r.emit("vector", lambda h, c=c, th=th:
                           h.tensor_tensor(XSTG[:, 0:512],
                                           Xv(c, th * 512, (th + 1) * 512),
                                           MU[:, :], OP.subtract))
                    r.emit("vector", lambda h, c=c, th=th:
                           h.tensor_tensor(XNv(c, th * 512, (th + 1) * 512),
                                           XSTG[:, 0:512], RS[:, :], OP.mult),
                           inc="sDVE")
                xn_half[th] = r.tok("sDVE")
                if th == 0:
                    r.wait_tok("scalar", r.tok("sDVE"))
            for q in range(4):
                pfree[("big", q)] = r.tok("sACT")
            return xn_half

        def ffn(xn_half):
            wslot = [0]
            stt_toks = {}
            spa_free = [None]
            load_plan = []
            for _sp in range(NSP):
                for _pb in range(NPB):
                    for _ib in range(NIB):
                        _i0 = _sp * SPW + _pb * PBW + _ib * IBW
                        load_plan.append((wg, _i0))
                        load_plan.append((wu, _i0))
            issued = [0]
            load_toks = {}

            def issue_wgu_upto(n):
                while issued[0] < min(n, len(load_plan)):
                    mat, i0 = load_plan[issued[0]]
                    slot = issued[0] % 2
                    load_toks[issued[0]] = (load_wgu(mat[:, i0:i0 + IBW], slot), slot)
                    issued[0] += 1
            down_read_tok = None
            h_free = None
            su_free = {}
            for sp in range(NSP):
                hsp_evacs = []
                pending = []
                sdl_pre = {}
                for dc in range(2):
                    cslot = (sp * NCH + dc) % 2
                    sdl_pre[dc] = load_wcol(wd[sp * SPW:(sp + 1) * SPW,
                                               dc * 128:(dc + 1) * 128], cslot)

                def emit_sparsify(ts, h_tok):
                    base = (ts - 1) * PBW
                    Hs = H[:, base:base + 2 * PBW]
                    Hr = Hs.rearrange("p (m r) -> p r m", r=4)
                    Ar = SPA[:, :].rearrange("p (m r) -> p r m", r=4)

                    def m(k):
                        return SPM[:, k * 256:(k + 1) * 256]

                    r.wait_tok("scalar", h_tok)
                    r.wait_tok("scalar", spa_free[0])
                    abs_tok = r.emit("scalar", lambda h, Hs=Hs:
                                     h.activation(out=SPA[:, :], in_=Hs,
                                                  func=AF.Abs), inc="sACT")
                    r.wait_tok("vector", abs_tok)
                    r.emit("vector", lambda h, Ar=Ar, m=m:
                           h.tensor_tensor(m(0), Ar[:, 0, :], Ar[:, 1, :], OP.max))
                    r.emit("vector", lambda h, Ar=Ar, m=m:
                           h.tensor_tensor(m(1), Ar[:, 0, :], Ar[:, 1, :], OP.min))
                    r.emit("vector", lambda h, Ar=Ar, m=m:
                           h.tensor_tensor(m(2), Ar[:, 2, :], Ar[:, 3, :], OP.max))
                    r.emit("vector", lambda h, Ar=Ar, m=m:
                           h.tensor_tensor(m(3), Ar[:, 2, :], Ar[:, 3, :], OP.min))
                    r.emit("vector", lambda h, m=m:
                           h.tensor_tensor(m(0), m(0), m(2), OP.min))
                    r.emit("vector", lambda h, m=m:
                           h.tensor_tensor(m(1), m(1), m(3), OP.max))
                    r.emit("vector", lambda h, m=m:
                           h.tensor_tensor(m(2), m(0), m(1), OP.max))
                    sp_tok = None
                    for rr in range(4):
                        r.emit("vector", lambda h, Ar=Ar, rr=rr, m=m:
                               h.tensor_tensor(MSK[:, :], Ar[:, rr, :], m(2),
                                               OP.is_ge))
                        sp_tok = r.emit("vector", lambda h, Hr=Hr, rr=rr:
                                        h.tensor_tensor(Hr[:, rr, :], Hr[:, rr, :],
                                                        MSK[:, :], OP.mult),
                                        inc="sDVE" if rr == 3 else None)
                    spa_free[0] = sp_tok
                    return sp_tok

                def emit_transpose(tpb, ts, sp_tok):
                    r.wait_tok("tensor", sp_tok)
                    half = ts % 2
                    pwait("tensor", ("ptr", half))
                    if down_read_tok is not None:
                        r.wait_tok("vector", down_read_tok)
                    PTR = pt["PTR"]
                    ttk = None
                    for k in range(4):
                        ttk = r.emit("tensor", lambda h, ts=ts, k=k, half=half, PTR=PTR:
                                     h.transpose(PTR[:, half * 1024 + k * 128:
                                                     half * 1024 + (k + 1) * 128],
                                                 H[:, ts * PBW + k * 128:
                                                   ts * PBW + (k + 1) * 128],
                                                 ID16[:, :]),
                                     inc="sPE" if k == 3 else None)
                    r.wait_tok("scalar", ttk)
                    c0 = tpb * 4
                    dstv = HSP[:].rearrange("p (c t) -> p c t", c=16)[
                        :, c0:c0 + 4, ts * 128:(ts + 1) * 128]
                    srcv = PTR[:, half * 1024:half * 1024 + 512].rearrange(
                        "p (k t) -> p k t", k=4)
                    ev = r.emit("scalar", lambda h, dstv=dstv, srcv=srcv:
                                h.copy(dstv, srcv), inc="sACT")
                    pfree[("ptr", half)] = ev
                    hsp_evacs.append(ev)

                for pb in range(NPB):
                    for ib in range(NIB):
                        bk = ((sp * NPB) + pb) * NIB + ib
                        issue_wgu_upto(2 * bk + 2)
                        sgate, gslot = load_toks[2 * bk]
                        sup, uslot = load_toks[2 * bk + 1]
                        r.wait(eng="tensor", key=sgate[0], val=sgate[1])
                        silu_toks = {}
                        ibpar = (pb * NIB + ib) % 2
                        SUb = SU if ibpar == 0 else SCB
                        gate_banks = [(("big", 0), big(0)), (("big", 1), big(1)),
                                      (("pc", 0), PC0)]
                        for ts in range(8):
                            r.wait_tok("tensor", xn_half[ts // 4])
                            gkey, gps = gate_banks[ts % 3]
                            pwait("tensor", gkey)
                            for c in range(NCH):
                                r.emit("tensor", lambda h, c=c, ts=ts, gps=gps, gslot=gslot:
                                       h.matmul(gps[:, 0:IBW],
                                                XNv(c, ts * 128, (ts + 1) * 128),
                                                WGU[:, gslot * NCH * IBW + c * IBW:
                                                    gslot * NCH * IBW + (c + 1) * IBW],
                                                start=(c == 0), stop=(c == NCH - 1)),
                                       inc="sPE" if c == NCH - 1 else None)
                            gt = r.tok("sPE")
                            if pending:
                                emit_transpose(*pending.pop(0))
                            r.wait_tok("scalar", gt)
                            r.wait_tok("scalar", su_free.get((ibpar, ts)))
                            stk = r.emit("scalar", lambda h, ts=ts, gps=gps, SUb=SUb:
                                         h.activation(out=SUb[:, ts * IBW:(ts + 1) * IBW],
                                                      in_=gps[:, 0:IBW],
                                                      func=AF.Silu), inc="sACT")
                            silu_toks[ts] = stk
                            pfree[gkey] = stk
                        wgu_slot_tok[gslot] = r.tok("sPE")
                        if ib == 0:
                            h_free = r.tok("sPE")
                        r.wait(eng="tensor", key=sup[0], val=sup[1])
                        last_ib = ib == NIB - 1
                        up_banks = [(("big", 2), big(2)), (("big", 3), big(3)),
                                    (("pc", 1), PC1)]
                        for ts in range(8):
                            ukey, ups = up_banks[ts % 3]
                            pwait("tensor", ukey)
                            for c in range(NCH):
                                r.emit("tensor", lambda h, c=c, ts=ts, ups=ups, uslot=uslot:
                                       h.matmul(ups[:, 0:IBW],
                                                XNv(c, ts * 128, (ts + 1) * 128),
                                                WGU[:, uslot * NCH * IBW + c * IBW:
                                                    uslot * NCH * IBW + (c + 1) * IBW],
                                                start=(c == 0), stop=(c == NCH - 1)),
                                       inc="sPE" if c == NCH - 1 else None)
                            ut = r.tok("sPE")
                            r.wait_tok("vector", ut)
                            r.wait_tok("vector", silu_toks[ts])
                            if h_free is not None:
                                r.wait_tok("vector", h_free)
                            htk = r.emit("vector", lambda h, ts=ts, ib=ib, ups=ups, SUb=SUb:
                                         h.tensor_tensor(
                                             H[:, ts * PBW + ib * IBW:
                                               ts * PBW + (ib + 1) * IBW],
                                             SUb[:, ts * IBW:(ts + 1) * IBW],
                                             ups[:, 0:IBW], OP.mult), inc="sDVE")
                            su_free[(ibpar, ts)] = htk
                            pfree[ukey] = htk
                            if last_ib and ts % 2 == 1:
                                ptok = emit_sparsify(ts, htk)
                                pending.append((pb, ts - 1, ptok))
                                pending.append((pb, ts, ptok))
                        wgu_slot_tok[uslot] = r.tok("sPE")
                    if pb == NPB - 1:
                        while pending:
                            emit_transpose(*pending.pop(0))
                        h_free = r.tok("sPE")
                hsp_done = hsp_evacs[-1]
                for dc in range(NCH):
                    if dc == 8:
                        issue_wgu_upto(2 * ((sp + 1) * NPB * NIB) + 2)
                    cslot = (sp * NCH + dc) % 2
                    if dc < 2:
                        sdl = sdl_pre[dc]
                    else:
                        sdl = load_wcol(wd[sp * SPW:(sp + 1) * SPW,
                                           dc * 128:(dc + 1) * 128], cslot)
                    r.wait(eng="tensor", key=sdl[0], val=sdl[1])
                    r.wait_tok("tensor", hsp_done)
                    down_banks = [(("pc", 0), PC0), (("pc", 1), PC1),
                                  (("big", 0), big(0)), (("big", 1), big(1))]
                    for th in range(2):
                        dkey, pc = down_banks[(dc * 2 + th) % 4]
                        pwait("tensor", dkey)
                        for c in range(NCH):
                            r.emit("tensor", lambda h, pc=pc, c=c, th=th, cslot=cslot:
                                   h.matmul(pc[:, :],
                                            WCOL[:, cslot * 2048 + c * 128:
                                                 cslot * 2048 + (c + 1) * 128],
                                            HSPv(c, th * 512, (th + 1) * 512),
                                            start=(c == 0), stop=(c == NCH - 1)),
                                   inc="sPE" if c == NCH - 1 else None)
                        dt = r.tok("sPE")
                        r.wait_tok("vector", dt)
                        stk = r.emit("vector", lambda h, pc=pc, dc=dc, th=th:
                                     h.scalar_tensor_tensor(
                                         Xv(dc, th * 512, (th + 1) * 512), pc[:, :],
                                         0.5, Xv(dc, th * 512, (th + 1) * 512),
                                         OP.mult, OP.add), inc="sDVE")
                        pfree[dkey] = stk
                        stt_toks[(dc, th)] = stk
                    wcol_slot_tok[cslot] = r.tok("sPE")
                down_read_tok = r.tok("sPE")
            x_tok_map = {}
            for c in range(NCH):
                a, b = stt_toks[(c, 0)], stt_toks[(c, 1)]
                x_tok_map[c] = b if b[1] >= a[1] else a
            return x_tok_map

        def attention(xn_half):
            pq_free = {}
            scf_free = None
            pq2_gate = [None]
            qb_gate = {0: None, 1: None}

            def qkv_chunk(oc, dest_kind, dest_slot):
                nonlocal pq_free, scf_free
                cslot = oc % 2
                sload = load_wcol(wqkv[:, oc * 128:(oc + 1) * 128], cslot)
                r.wait(eng="tensor", key=sload[0], val=sload[1])
                last = None
                for th in range(2):
                    PQb = pt["PQ"] if (th == 0 or "PQ2" not in pt) else pt["PQ2"]
                    key = "PQ" if PQb is pt["PQ"] else "PQ2"
                    r.wait_tok("tensor", xn_half[th])
                    for tk in pq_free.get(key, []):
                        r.wait_tok("tensor", tk)
                    if key == "PQ2" and pq2_gate[0] is not None:
                        r.wait_tok("tensor", pq2_gate[0])
                        pq2_gate[0] = None
                    for c in range(NCH):
                        r.emit("tensor", lambda h, c=c, th=th, cslot=cslot, PQb=PQb:
                               h.matmul(PQb[:, :],
                                        WCOL[:, cslot * 2048 + c * 128:
                                             cslot * 2048 + (c + 1) * 128],
                                        XNv(c, th * 512, (th + 1) * 512),
                                        start=(c == 0), stop=(c == NCH - 1)),
                               inc="sPE" if c == NCH - 1 else None)
                    qt = r.tok("sPE")
                    if dest_kind == "v":
                        r.wait_tok("scalar", qt)
                        r.wait_tok("scalar", scf_free)
                        vct = r.emit("scalar", lambda h, PQb=PQb:
                                     h.activation(out=SCF[:, 0:512], in_=PQb[:, :],
                                                  func=AF.Copy), inc="sACT")
                        pq_free[key] = [vct]
                        r.wait_tok("tensor", vct)
                        pwait("tensor", ("ptr", 0))
                        vtt = None
                        PTRV = pt["PTRV"]
                        for k in range(4):
                            vtt = r.emit("tensor", lambda h, k=k, PTRV=PTRV:
                                         h.transpose(PTRV[:, k * 128:(k + 1) * 128],
                                                     SCF[:, k * 128:(k + 1) * 128],
                                                     ID16[:, :]),
                                         inc="sPE" if k == 3 else None)
                        r.wait_tok("vector", vtt)
                        last = r.emit("vector", lambda h, dest_slot=dest_slot, th=th:
                                      h.tensor_copy(
                                          SCB[:, dest_slot * 1024 + th * 512:
                                              dest_slot * 1024 + (th + 1) * 512],
                                          pt["PTRV"][:, 0:512]), inc="sDVE")
                        pfree[("ptr", 0)] = last
                        scf_free = last
                    elif "rot" not in ATTN_PARTS:
                        r.wait_tok("scalar", qt)
                        last = r.emit("scalar", lambda h, dest_slot=dest_slot, th=th:
                                      h.activation(out=SCB[:, dest_slot * 1024 + th * 512:
                                                       dest_slot * 1024 + (th + 1) * 512],
                                                   in_=pt["PQ"][:, :], func=AF.Copy),
                                      inc="sACT")
                        pq_free = [last]
                    else:
                        r.wait_tok("scalar", qt)
                        r.wait_tok("scalar", scf_free)
                        r.emit("scalar", lambda h, PQb=PQb:
                               h.copy(SCF[0:64, 512:1024], PQb[64:128, :]))
                        swt = r.emit("scalar", lambda h, PQb=PQb:
                                     h.copy(SCF[64:128, 512:1024], PQb[0:64, :]),
                                     inc="sACT")
                        r.wait_tok("vector", qt)
                        r.wait_tok("vector", swt)
                        r.emit("vector", lambda h, th=th, PQb=PQb:
                               h.tensor_tensor(SCF[:, 0:512], PQb[:, :],
                                               TBA[:, th * 512:(th + 1) * 512],
                                               OP.mult))
                        r.emit("vector", lambda h, th=th:
                               h.tensor_tensor(SCF[:, 512:1024], SCF[:, 512:1024],
                                               TBB[:, th * 512:(th + 1) * 512],
                                               OP.mult))
                        if dest_kind == "k":
                            dv = SCB[:, dest_slot * 1024 + th * 512:
                                     dest_slot * 1024 + (th + 1) * 512]
                        else:
                            r.wait_tok("vector", qb_gate[dest_slot % 2])
                            dv = QB[:, (dest_slot % 2) * 1024 + th * 512:
                                    (dest_slot % 2) * 1024 + (th + 1) * 512]
                        last = r.emit("vector", lambda h, dv=dv:
                                      h.tensor_tensor(dv, SCF[:, 0:512],
                                                      SCF[:, 512:1024], OP.add),
                                      inc="sDVE")
                        pq_free[key] = [last]
                        scf_free = last
                wcol_slot_tok[cslot] = r.tok("sPE")
                return last

            ptrv_stack = contextlib.ExitStack()
            pt["PTRV"] = ptrv_stack.enter_context(
                nc.psum_tensor("PTRVa", [128, 1024], F16))
            kv_send = []
            for kv in range(2):
                if "k%d" % kv not in ATTN_PARTS:
                    continue
                ktok = qkv_chunk(16 + kv, "k", kv)
                if "send" not in ATTN_PARTS:
                    continue
                r.wait_tok("sync", ktok)
                kv_send.append(r.emit("sync", lambda h, kv=kv:
                                      h.dma_start(out=cc_send[kv * 128:(kv + 1) * 128, :],
                                                  in_=SCB[:, kv * 1024:(kv + 1) * 1024]),
                                      inc="sIO", n=16))
            for kv in range(2):
                if "v%d" % kv not in ATTN_PARTS:
                    continue
                if "send" in ATTN_PARTS:
                    r.wait_tok("vector", kv_send[kv])
                vtok = qkv_chunk(18 + kv, "v", kv)
                if "send" not in ATTN_PARTS:
                    continue
                r.wait_tok("sync", vtok)
                kv_send.append(r.emit("sync", lambda h, kv=kv:
                                      h.dma_start(
                                          out=cc_send[256 + kv * 128:
                                                      256 + (kv + 1) * 128, :],
                                          in_=SCB[:, kv * 1024:(kv + 1) * 1024]),
                                      inc="sIO", n=16))
            if "cc" not in ATTN_PARTS:
                return None
            ptrv_stack.close()
            pq2_stack = contextlib.ExitStack()
            pt["PQ2"] = pq2_stack.enter_context(
                nc.psum_tensor("PQ2a", [128, 512], F32))
            pq2_gate[0] = r.tok("sDVE")
            r.wait(eng="gpsimd", key="sIO", val=r.cnt["sIO"])
            r.emit("gpsimd", lambda h:
                   h.collective_compute("AllGather", OP.bypass,
                                        replica_groups=[[0, 1], [2, 3], [4, 5], [6, 7]],
                                        ins=[cc_send.ap().opt()],
                                        outs=[cc_recv.ap().opt()]), inc="sCC")

            def load_kv(kv, gate_tok):
                r.wait(eng="sync", key="sCC", val=1)
                r.wait_tok("sync", gate_tok)
                for dst, row in ((KG[:, 0:1024], kv * 128),
                                 (KG[:, 1024:2048], 512 + kv * 128),
                                 (VT[:, 0:1024], 256 + kv * 128),
                                 (VT[:, 1024:2048], 768 + kv * 128)):
                    r.emit("sync", lambda h, dst=dst, row=row:
                           h.dma_start(out=dst, in_=cc_recv[row:row + 128, :]),
                           inc="sIO", n=16)
                return r.tok("sIO")

            kv_tok = load_kv(0, None)
            if "q" not in ATTN_PARTS:
                return None
            q_toks = {0: qkv_chunk(0, "q", 0), 1: qkv_chunk(1, "q", 1)}
            dn_o_free = None
            a_last = None
            head_pass_end = {}

            if "heads" not in ATTN_PARTS:
                return None
            for hq in range(NH):
                if hq == 8:
                    kv_tok = load_kv(1, head_pass_end[7])
                r.wait(eng="tensor", key=kv_tok[0], val=kv_tok[1])
                r.wait_tok("tensor", q_toks[hq])
                ebf_free = {0: None, 1: None}
                for b in range(2):
                    tiles = []
                    for mm in range(4 * b + 4):
                        for pi in range(2):
                            N = 512 if mm < 4 * b else 512 - 128 * (mm - 4 * b)
                            tiles.append((pi, mm, N))
                    batches = []
                    cur, off = [], 0
                    for (pi, mm, N) in tiles:
                        noff = off
                        if noff % 512 + N > 512:
                            noff = (noff // 512 + 1) * 512
                        if noff + N > 1024:
                            batches.append(cur)
                            cur, noff = [], 0
                        cur.append((pi, mm, N, noff))
                        off = noff + N
                    if cur:
                        batches.append(cur)
                    first = [True]
                    pend = []

                    def emit_dnpv(batch, eslot, ext, mask_tok, last_batch):
                        r.wait_tok("tensor", ext)
                        if mask_tok is not None:
                            r.wait_tok("tensor", mask_tok)
                        if first[0]:
                            pwait("tensor", ("pc", 0), ("pc", 1))
                            if dn_o_free is not None:
                                r.wait_tok("tensor", dn_o_free)
                        pvt = None
                        for j, (pi, mm, N, off) in enumerate(batch):
                            is_first = first[0] and j == 0
                            is_last = last_batch and j == len(batch) - 1
                            r.emit("tensor",
                                   lambda h, N=N, off=off, eslot=eslot,
                                   is_first=is_first, is_last=is_last:
                                   h.matmul(PC0[:, 512 - N:512], ONES[:, :],
                                            EBF[:, eslot * 1024 + off:
                                                eslot * 1024 + off + N],
                                            start=is_first, stop=is_last))
                            pvt = r.emit("tensor",
                                         lambda h, pi=pi, mm=mm, N=N, off=off,
                                         eslot=eslot, is_first=is_first,
                                         is_last=is_last:
                                         h.matmul(PC1[:, 512 - N:512],
                                                  VT[:, (pi * 8 + mm) * 128:
                                                     (pi * 8 + mm + 1) * 128],
                                                  EBF[:, eslot * 1024 + off:
                                                      eslot * 1024 + off + N],
                                                  start=is_first, stop=is_last),
                                         inc="sPE" if j == len(batch) - 1 else None)
                        ebf_free[eslot] = pvt
                        first[0] = False

                    for bi, batch in enumerate(batches):
                        sb = bi % 2
                        eslot = bi % 2
                        pwait("tensor", ("big", 2 * sb), ("big", 2 * sb + 1))
                        W = max(off2 + N2 for (pi2, mm2, N2, off2) in batch)
                        spt = None
                        for j, (pi, mm, N, off) in enumerate(batch):
                            qlo = b * 512 + (512 - N)
                            spt = r.emit(
                                "tensor",
                                lambda h, pi=pi, mm=mm, N=N, off=off, sb=sb, hq=hq,
                                qlo=qlo:
                                h.matmul(bighalf(sb)[:, off:off + N],
                                         KG[:, pi * 1024 + mm * 128:
                                            pi * 1024 + (mm + 1) * 128],
                                         QB[:, (hq % 2) * 1024 + qlo:
                                            (hq % 2) * 1024 + qlo + N],
                                         start=True, stop=True),
                                inc="sPE" if j == len(batch) - 1 else None)
                        r.wait_tok("scalar", spt)
                        r.wait_tok("scalar", ebf_free[eslot])
                        r.emit("scalar", lambda h, sb=sb, W=W:
                               h.activation(out=ESB[:, 0:W], in_=bighalf(sb)[:, 0:W],
                                            func=AF.Tanh, scale=SCALE / CAP))
                        ext = r.emit("scalar", lambda h, eslot=eslot, W=W:
                                     h.activation(
                                         out=EBF[:, eslot * 1024:eslot * 1024 + W],
                                         in_=ESB[:, 0:W], func=AF.Exp, scale=CAP),
                                     inc="sACT")
                        pfree[("big", 2 * sb)] = ext
                        pfree[("big", 2 * sb + 1)] = ext
                        mask_tok = None
                        diag = [t for t in batch if t[1] >= 4 * b]
                        if diag:
                            r.wait_tok("vector", ext)
                            for j, (pi, mm, N, off) in enumerate(diag):
                                mask_tok = r.emit(
                                    "vector",
                                    lambda h, eslot=eslot, off=off, pi=pi:
                                    h.tensor_tensor(
                                        EBF[:, eslot * 1024 + off:
                                            eslot * 1024 + off + 128],
                                        EBF[:, eslot * 1024 + off:
                                            eslot * 1024 + off + 128],
                                        (MP0 if pi == 0 else MP1)[:, :], OP.mult),
                                    inc="sDVE" if j == len(diag) - 1 else None)
                        pend.append((batch, eslot, ext, mask_tok,
                                     bi == len(batches) - 1))
                        if len(pend) > 1:
                            emit_dnpv(*pend.pop(0))
                    while pend:
                        emit_dnpv(*pend.pop(0))
                    r.wait_tok("vector", r.tok("sPE"))
                    r.emit("vector", lambda h: h.reciprocal(RC[:, :], PC0[:, :]))
                    aev = r.emit("vector", lambda h, hq=hq, b=b:
                                 h.tensor_tensor(HSPv(hq, b * 512, (b + 1) * 512),
                                                 PC1[:, :], RC[:, :], OP.mult),
                                 inc="sDVE")
                    dn_o_free = aev
                    a_last = aev
                head_pass_end[hq] = r.tok("sPE")
                qb_gate[hq % 2] = head_pass_end[hq]
                if hq + 2 < NH:
                    q_toks[hq + 2] = qkv_chunk(hq + 2, "q", hq + 2)
            if "wo" not in ATTN_PARTS:
                return None
            stt_toks = {}
            for dc in range(NCH):
                cslot = dc % 2
                sload = load_wcol(wo[:, dc * 128:(dc + 1) * 128], cslot)
                r.wait(eng="tensor", key=sload[0], val=sload[1])
                r.wait_tok("tensor", a_last)
                if dn_o_free is not None:
                    r.wait_tok("tensor", dn_o_free)
                wo_banks = [(("pc", 0), PC0), (("pc", 1), PC1),
                            (("big", 0), big(0)), (("big", 1), big(1))]
                for th in range(2):
                    wkey, pc = wo_banks[(dc * 2 + th) % 4]
                    pwait("tensor", wkey)
                    for c in range(NCH):
                        r.emit("tensor", lambda h, pc=pc, c=c, th=th, cslot=cslot:
                               h.matmul(pc[:, :],
                                        WCOL[:, cslot * 2048 + c * 128:
                                             cslot * 2048 + (c + 1) * 128],
                                        HSPv(c, th * 512, (th + 1) * 512),
                                        start=(c == 0), stop=(c == NCH - 1)),
                               inc="sPE" if c == NCH - 1 else None)
                    wt = r.tok("sPE")
                    r.wait_tok("vector", wt)
                    stk = r.emit("vector", lambda h, pc=pc, dc=dc, th=th:
                                 h.scalar_tensor_tensor(
                                     Xv(dc, th * 512, (th + 1) * 512), pc[:, :], 1.0,
                                     Xv(dc, th * 512, (th + 1) * 512),
                                     OP.mult, OP.add), inc="sDVE")
                    pfree[wkey] = stk
                    stt_toks[(dc, th)] = stk
                wcol_slot_tok[cslot] = r.tok("sPE")
            x_tok_map = {}
            for c in range(NCH):
                a, b = stt_toks[(c, 0)], stt_toks[(c, 1)]
                x_tok_map[c] = b if b[1] >= a[1] else a
            pq2_stack.close()
            pt.pop("PQ2", None)
            return x_tok_map

        x_map = x_ready
        if "ffn1" in PHASES:
            xa = layernorm(x_map)
            with nc.psum_tensor("PTRa", [128, 2048], F16) as _p:
                pt["PTR"] = _p
                x_map = ffn(xa)
        if "attn" in PHASES:
            xa = layernorm(x_map)
            with nc.psum_tensor("PQa", [128, 512], F32) as _pq:
                pt["PQ"] = _pq
                xm2 = attention(xa)
            if xm2 is not None:
                x_map = xm2
        if "ffn2" in PHASES:
            xa = layernorm(x_map)
            with nc.psum_tensor("PTRb", [128, 2048], F16) as _p:
                pt["PTR"] = _p
                x_map = ffn(xa)

        stg_free = {0: None, 1: None}
        for ts in range(8):
            for dh in range(2):
                for k4 in range(2):
                    half = (ts * 4 + dh * 2 + k4) % 2
                    q = k4
                    pwait("tensor", ("big", q))
                    tk = None
                    for k in range(4):
                        c = dh * 8 + k4 * 4 + k
                        r.wait_tok("tensor", x_map.get(c))
                        tk = r.emit("tensor", lambda h, c=c, ts=ts, q=q, k=k:
                                    h.transpose(big(q)[:, k * 128:(k + 1) * 128],
                                                Xv(c, ts * 128, (ts + 1) * 128),
                                                ID32[:, :]),
                                    inc="sPE" if k == 3 else None)
                    r.wait_tok("vector", tk)
                    r.wait_tok("vector", stg_free[half])
                    ev = r.emit("vector", lambda h, half=half, q=q:
                                h.tensor_copy(XSTG[:, half * 512:(half + 1) * 512],
                                              big(q)), inc="sDVE")
                    pfree[("big", q)] = ev
                    r.wait_tok("sync", ev)
                    dt = r.emit("sync", lambda h, ts=ts, dh=dh, k4=k4, half=half:
                                h.dma_start(out=out[ts * 128:(ts + 1) * 128,
                                                    dh * 1024 + k4 * 512:
                                                    dh * 1024 + (k4 + 1) * 512],
                                            in_=XSTG[:, half * 512:(half + 1) * 512]),
                                inc="sIO", n=16)
                    stg_free[half] = dt
        r.wait(eng="sync", key="sIO", val=r.cnt["sIO"])

        def make_body(ename):
            ops = r.ops[ename]

            def body(h):
                for op in ops:
                    if op[0] == "w":
                        h.wait_ge(sems[op[1]], op[2])
                    else:
                        inst = op[1](h)
                        if op[2] is not None:
                            inst.then_inc(sems[op[2]], op[3])
            return body

        blk.tensor(make_body("tensor"))
        blk.vector(make_body("vector"))
        blk.scalar(make_body("scalar"))
        blk.gpsimd(make_body("gpsimd"))
        blk.sync(make_body("sync"))

    return nc



def _host_tables(parity):
    inv_freq = 1.0 / (10000.0 ** (np.arange(0, HD, 2, dtype=np.float64) / HD))
    pos = 2.0 * np.arange(NT, dtype=np.float64) + parity
    fr = np.outer(inv_freq, pos)
    c = np.cos(fr)
    s = np.sin(fr)
    tbl_a = np.concatenate([c, c], axis=0).astype(np.float16)
    tbl_b = np.concatenate([-s, s], axis=0).astype(np.float16)
    return tbl_a, tbl_b


def kernel(L0_attn_hidden_states, L0_attn_Wqkv, L0_attn_Wo,
           L0_ffn_w_gate, L0_ffn_w_up, L0_ffn_w_down):
    x = np.asarray(L0_attn_hidden_states)
    wqkv = np.ascontiguousarray(np.asarray(L0_attn_Wqkv), dtype=np.float32)
    wov = np.ascontiguousarray(np.asarray(L0_attn_Wo), dtype=np.float32)
    wgt = np.ascontiguousarray(np.asarray(L0_ffn_w_gate), dtype=np.float32)
    wup = np.ascontiguousarray(np.asarray(L0_ffn_w_up), dtype=np.float32)
    wdn = np.ascontiguousarray(np.asarray(L0_ffn_w_down), dtype=np.float32)

    jj = np.arange(128)[:, None]
    ii = np.arange(128)[None, :]
    mp0 = (jj <= ii).astype(ml_dtypes.bfloat16)
    mp1_even = (jj < ii).astype(ml_dtypes.bfloat16)
    mp1_odd = (jj <= ii).astype(ml_dtypes.bfloat16)
    id32 = np.eye(128, dtype=np.float32)
    id16 = np.eye(128, dtype=np.float16)
    ones_bf = np.ones((128, 128), dtype=ml_dtypes.bfloat16)

    nc = build_program()
    in_maps = []
    for core in range(8):
        b = core // 2
        parity = core % 2
        tbl_a, tbl_b = _host_tables(parity)
        in_maps.append({
            "x": np.ascontiguousarray(x[b, parity::2, :], dtype=np.float32),
            "wqkv": wqkv, "wo": wov, "wg": wgt, "wu": wup, "wd": wdn,
            "tbl_a": tbl_a, "tbl_b": tbl_b,
            "mask_p0": mp0, "mask_p1": (mp1_even if parity == 0 else mp1_odd),
            "ident32": id32, "ident16": id16, "ones_bf": ones_bf,
            "eps32": np.full((128, 1), LN_EPS, dtype=np.float32),
        })
    res = run_bass_kernel_spmd(nc, in_maps, core_ids=list(range(8)))
    outp = np.empty((4, 2048, 2048), dtype=np.float32)
    for core in range(8):
        b = core // 2
        parity = core % 2
        outp[b, parity::2, :] = res.results[core]["out"]
    return outp


# version 18
# speedup vs baseline: 1.0474x; 1.0474x over previous
import numpy as np
import ml_dtypes
import contextlib
import concourse.bass as bass
import concourse.mybir as mybir
from concourse.bass_utils import run_bass_kernel_spmd
from concourse.alu_op_type import AluOpType as OP

F32 = mybir.dt.float32
BF16 = mybir.dt.bfloat16
F16 = mybir.dt.float16
AF = mybir.ActivationFunctionType

D = 2048
NCH = 16
NT = 1024
NH = 16
HD = 128
II = 8192
SCALE = 0.08838834764831845
CAP = 30.0
LN_EPS = 1e-5
QKV_OUT = 2560

IBW = 256
PBW = 512
SPW = 2048
NSP = II // SPW
NPB = SPW // PBW
NIB = PBW // IBW

PHASES = ("ffn1", "attn", "ffn2")
ATTN_PARTS = ("k0", "k1", "v0", "v1", "send", "cc", "q", "heads", "wo", "rot")

ENGS = ("tensor", "vector", "scalar", "gpsimd", "sync")


class Rec:

    def __init__(self, sems):
        self.ops = {e: [] for e in ENGS}
        self.sems = sems
        self.cnt = {k: 0 for k in sems}
        self.last_wait = {}

    def wait(self, eng, key, val):
        if val is None or val <= 0:
            return
        if self.last_wait.get((eng, key), 0) >= val:
            return
        self.last_wait[(eng, key)] = val
        self.ops[eng].append(("w", key, val))

    def wait_tok(self, eng, tok):
        if tok is not None:
            self.wait(eng, tok[0], tok[1])

    def emit(self, eng, fn, inc=None, n=1):
        self.ops[eng].append(("c", fn, inc, n))
        if inc is not None:
            self.cnt[inc] += n
            return (inc, self.cnt[inc])
        return None

    def tok(self, key):
        return (key, self.cnt[key])


def build_program():
    nc = bass.Bass("TRN2", target_bir_lowering=False)

    x_in = nc.dram_tensor("x", [NT, D], F32, kind="ExternalInput").ap()
    wqkv = nc.dram_tensor("wqkv", [D, QKV_OUT], F32, kind="ExternalInput").ap()
    wo = nc.dram_tensor("wo", [D, D], F32, kind="ExternalInput").ap()
    wg = nc.dram_tensor("wg", [D, II], F32, kind="ExternalInput").ap()
    wu = nc.dram_tensor("wu", [D, II], F32, kind="ExternalInput").ap()
    wd = nc.dram_tensor("wd", [II, D], F32, kind="ExternalInput").ap()
    tbl_a_in = nc.dram_tensor("tbl_a", [128, NT], F16, kind="ExternalInput").ap()
    tbl_b_in = nc.dram_tensor("tbl_b", [128, NT], F16, kind="ExternalInput").ap()
    mask_p0_in = nc.dram_tensor("mask_p0", [128, 128], BF16, kind="ExternalInput").ap()
    mask_p1_in = nc.dram_tensor("mask_p1", [128, 128], BF16, kind="ExternalInput").ap()
    ident32_in = nc.dram_tensor("ident32", [128, 128], F32, kind="ExternalInput").ap()
    ident16_in = nc.dram_tensor("ident16", [128, 128], F16, kind="ExternalInput").ap()
    ones_in = nc.dram_tensor("ones_bf", [128, 128], BF16, kind="ExternalInput").ap()
    eps_in = nc.dram_tensor("eps32", [128, 1], F32, kind="ExternalInput").ap()
    out = nc.dram_tensor("out", [NT, D], F32, kind="ExternalOutput").ap()
    cc_send = nc.dram_tensor("cc_send", [512, NT], BF16)
    cc_recv = nc.dram_tensor("cc_recv", [1024, NT], BF16)

    st = contextlib.ExitStack()
    with st:
        X = st.enter_context(nc.sbuf_tensor("Xst", [128, NCH * NT], F32))
        XN = st.enter_context(nc.sbuf_tensor("XNst", [128, NCH * NT], BF16))
        HSP = st.enter_context(nc.sbuf_tensor("HSPst", [128, 16 * NT], BF16))
        H = st.enter_context(nc.sbuf_tensor("Hst", [128, 8 * PBW], F16))
        WGU = st.enter_context(nc.sbuf_tensor("WGUst", [128, 2 * NCH * IBW], BF16))
        WCOL = st.enter_context(nc.sbuf_tensor("WCOLst", [128, 2 * 2048], BF16))
        SU = st.enter_context(nc.sbuf_tensor("SUst", [128, 8 * IBW], F16))
        SPA = st.enter_context(nc.sbuf_tensor("SPAst", [128, 2 * PBW], F16))
        SPM = st.enter_context(nc.sbuf_tensor("SPMst", [128, 4 * 256], F16))
        ESB = st.enter_context(nc.sbuf_tensor("ESBst", [128, 1024], F16))
        EBF = st.enter_context(nc.sbuf_tensor("EBFst", [128, 2 * 1024], BF16))
        KG = st.enter_context(nc.sbuf_tensor("KGst", [128, 2048], BF16))
        VT = st.enter_context(nc.sbuf_tensor("VTst", [128, 2048], BF16))
        TBA = st.enter_context(nc.sbuf_tensor("TBAst", [128, NT], F16))
        TBB = st.enter_context(nc.sbuf_tensor("TBBst", [128, NT], F16))
        MU = st.enter_context(nc.sbuf_tensor("MUst", [128, 512], BF16))
        RS = st.enter_context(nc.sbuf_tensor("RSst", [128, 512], F32))
        SCF = st.enter_context(nc.sbuf_tensor("SCFst", [128, 2 * 512], F16))
        SCB = st.enter_context(nc.sbuf_tensor("SCBst", [128, 2 * 1024], BF16))
        QB = st.enter_context(nc.sbuf_tensor("QBst", [128, 2 * 1024], BF16))
        XSTG = st.enter_context(nc.sbuf_tensor("XSTGst", [128, 1024], F32))
        ID32 = st.enter_context(nc.sbuf_tensor("ID32st", [128, 128], F32))
        ID16 = st.enter_context(nc.sbuf_tensor("ID16st", [128, 128], F16))
        ONES = st.enter_context(nc.sbuf_tensor("ONESst", [128, 128], BF16))
        MP0 = st.enter_context(nc.sbuf_tensor("MP0st", [128, 128], BF16))
        MP1 = st.enter_context(nc.sbuf_tensor("MP1st", [128, 128], BF16))
        EPS = st.enter_context(nc.sbuf_tensor("EPSst", [128, 1], F32))

        PBIG = st.enter_context(nc.psum_tensor("PBIG", [128, 2048], F32))
        PC0 = st.enter_context(nc.psum_tensor("PC0", [128, 512], F32))
        PC1 = st.enter_context(nc.psum_tensor("PC1", [128, 512], F32))
        pt = {}

        keys = ["sW0", "sW1", "sC0", "sC1", "sC2", "sX0", "sX1", "sIO", "sCC",
                "sPE", "sACT", "sDVE"]
        sems = {}
        for k in keys:
            sems[k] = st.enter_context(nc.semaphore(k))
        blk = st.enter_context(nc.Block())

        r = Rec(sems)
        pfree = {}

        def pwait(eng, *regions):
            for reg in regions:
                r.wait_tok(eng, pfree.get(reg))

        def Xv(c, a=0, b=NT):
            return X[:, c * NT + a: c * NT + b]

        def XNv(c, a=0, b=NT):
            return XN[:, c * NT + a: c * NT + b]

        def HSPv(c, a=0, b=NT):
            return HSP[:, c * NT + a: c * NT + b]

        def big(q):
            return PBIG[:, q * 512:(q + 1) * 512]

        def bighalf(j):
            return PBIG[:, j * 1024:(j + 1) * 1024]

        for dst, src in ((TBA, tbl_a_in), (TBB, tbl_b_in), (MP0, mask_p0_in),
                         (MP1, mask_p1_in), (ID32, ident32_in), (ID16, ident16_in),
                         (ONES, ones_in), (EPS, eps_in)):
            r.emit("sync", lambda h, d=dst, s=src: h.dma_start(out=d[:, :], in_=s),
                   inc="sIO", n=16)
        consts_tok = r.tok("sIO")
        for e in ("tensor", "vector", "scalar"):
            r.wait_tok(e, consts_tok)

        xin_evac = {}
        xstg_free = {0: None, 1: None}
        nx = {0: 0, 1: 0}
        it = [0]
        for dh in range(2):
            for k4 in range(2):
                for ts in range(8):
                    half = it[0] % 2
                    it[0] += 1
                    skey = "sX%d" % half
                    stg = XSTG[:, half * 512:(half + 1) * 512]
                    r.wait_tok("sync", xstg_free[half])
                    nx[half] += 1
                    r.emit("sync", lambda h, ts=ts, dh=dh, k4=k4, stg=stg:
                           h.dma_start(out=stg,
                                       in_=x_in[ts * 128:(ts + 1) * 128,
                                                dh * 1024 + k4 * 512:
                                                dh * 1024 + (k4 + 1) * 512]),
                           inc=skey, n=16)
                    r.wait(eng="tensor", key=skey, val=nx[half] * 16)
                    q = ts % 2
                    pwait("tensor", ("big", q))
                    t_pe = None
                    for k in range(4):
                        t_pe = r.emit("tensor", lambda h, q=q, k=k, stg=stg:
                                      h.transpose(big(q)[:, k * 128:(k + 1) * 128],
                                                  stg[:, k * 128:(k + 1) * 128],
                                                  ID32[:, :]),
                                      inc="sPE" if k == 3 else None)
                    xstg_free[half] = r.tok("sPE")
                    r.wait_tok("scalar", t_pe)
                    c0 = dh * 8 + k4 * 4
                    dstv = X[:].rearrange("p (c t) -> p c t", c=NCH)[
                        :, c0:c0 + 4, ts * 128:(ts + 1) * 128]
                    srcv = big(q).rearrange("p (k t) -> p k t", k=4)
                    ev = r.emit("scalar", lambda h, dstv=dstv, srcv=srcv:
                                h.copy(dstv, srcv), inc="sACT")
                    pfree[("big", q)] = ev
                    for c in range(c0, c0 + 4):
                        xin_evac[c] = ev

        x_ready = dict(xin_evac)

        wgu_slot_tok = {0: None, 1: None}
        wgu_loads = {0: 0, 1: 0}
        wcol_slot_tok = {0: None, 1: None}
        wcol_loads = {0: 0, 1: 0}
        wcol_ctr = [0]

        def load_wgu(src_ap, slot):
            skey = "sW%d" % slot
            r.wait_tok("gpsimd", wgu_slot_tok[slot])
            dst = WGU[:, slot * NCH * IBW:(slot + 1) * NCH * IBW].rearrange(
                "p (c i) -> p c i", c=NCH)
            src = src_ap.rearrange("(c p) i -> p c i", p=128)
            r.emit("gpsimd", lambda h, dst=dst, src=src: h.dma_start(out=dst, in_=src),
                   inc=skey, n=16)
            wgu_loads[slot] += 1
            return (skey, wgu_loads[slot] * 16)

        def load_wcol(src_ap, slot=None):
            slot = wcol_ctr[0] % 2
            wcol_ctr[0] += 1
            skey = "sC%d" % slot
            r.wait_tok("gpsimd", wcol_slot_tok[slot])
            dst = WCOL[:, slot * 2048:(slot + 1) * 2048].rearrange(
                "p (c i) -> p c i", c=NCH)
            src = src_ap.rearrange("(c p) i -> p c i", p=128)
            r.emit("gpsimd", lambda h, dst=dst, src=src: h.dma_start(out=dst, in_=src),
                   inc=skey, n=16)
            wcol_loads[slot] += 1
            return (skey, wcol_loads[slot] * 16, slot)

        def layernorm(x_tok_map):
            for q in range(4):
                pwait("tensor", ("big", q))
            stats_tok = {}
            xn_half = {}
            for c in range(NCH):
                buf = SCB if c % 2 == 0 else QB
                r.wait_tok("scalar", x_tok_map.get(c))
                if c >= 2:
                    r.wait_tok("scalar", stats_tok[c - 2])
                r.emit("scalar", lambda h, c=c, buf=buf: h.copy(buf[:, 0:1024], Xv(c)))
                ct = r.emit("scalar", lambda h, c=c, buf=buf:
                            h.activation(out=buf[:, 1024:2048], in_=Xv(c),
                                         func=AF.Square), inc="sACT")
                r.wait_tok("tensor", ct)
                for th in range(2):
                    r.emit("tensor", lambda h, c=c, th=th, buf=buf:
                           h.matmul(big(th), ONES[:, :],
                                    buf[:, th * 512:(th + 1) * 512],
                                    start=(c == 0), stop=(c == NCH - 1)))
                    r.emit("tensor", lambda h, c=c, th=th, buf=buf:
                           h.matmul(big(2 + th), ONES[:, :],
                                    buf[:, 1024 + th * 512: 1024 + (th + 1) * 512],
                                    start=(c == 0), stop=(c == NCH - 1)),
                           inc="sPE" if th == 1 else None)
                stats_tok[c] = r.tok("sPE")
            for th in range(2):
                r.wait_tok("scalar", stats_tok[NCH - 1])
                r.emit("scalar", lambda h, th=th:
                       h.activation(out=MU[:, :], in_=big(th), func=AF.Copy,
                                    scale=1.0 / D))
                e2t = r.emit("scalar", lambda h, th=th:
                             h.activation(out=XSTG[:, 0:512], in_=big(2 + th),
                                          func=AF.Copy, scale=1.0 / D), inc="sACT")
                r.wait_tok("vector", e2t)
                r.emit("vector", lambda h: h.tensor_tensor(
                    XSTG[:, 512:1024], MU[:, :], MU[:, :], OP.mult))
                vt = r.emit("vector", lambda h:
                            h.tensor_tensor(XSTG[:, 0:512], XSTG[:, 0:512],
                                            XSTG[:, 512:1024], OP.subtract), inc="sDVE")
                r.wait_tok("scalar", vt)
                r.emit("scalar", lambda h:
                       h.activation(out=XSTG[:, 512:1024], in_=XSTG[:, 0:512],
                                    func=AF.Ln, bias=EPS[:, 0:1]))
                sdt = r.emit("scalar", lambda h:
                             h.activation(out=RS[:, :], in_=XSTG[:, 512:1024],
                                          func=AF.Exp, scale=-0.5), inc="sACT")
                r.wait_tok("vector", sdt)
                for c in range(NCH):
                    r.emit("vector", lambda h, c=c, th=th:
                           h.tensor_tensor(XSTG[:, 0:512],
                                           Xv(c, th * 512, (th + 1) * 512),
                                           MU[:, :], OP.subtract))
                    r.emit("vector", lambda h, c=c, th=th:
                           h.tensor_tensor(XNv(c, th * 512, (th + 1) * 512),
                                           XSTG[:, 0:512], RS[:, :], OP.mult),
                           inc="sDVE")
                xn_half[th] = r.tok("sDVE")
                if th == 0:
                    r.wait_tok("scalar", r.tok("sDVE"))
            for q in range(4):
                pfree[("big", q)] = r.tok("sACT")
            return xn_half

        def ffn(xn_half):
            wslot = [0]
            stt_toks = {}
            spa_free = [None]
            load_plan = []
            for _sp in range(NSP):
                for _pb in range(NPB):
                    for _ib in range(NIB):
                        _i0 = _sp * SPW + _pb * PBW + _ib * IBW
                        load_plan.append((wg, _i0))
                        load_plan.append((wu, _i0))
            issued = [0]
            load_toks = {}

            def issue_wgu_upto(n):
                while issued[0] < min(n, len(load_plan)):
                    mat, i0 = load_plan[issued[0]]
                    slot = issued[0] % 2
                    load_toks[issued[0]] = (load_wgu(mat[:, i0:i0 + IBW], slot), slot)
                    issued[0] += 1
            down_read_tok = None
            h_free = None
            su_free = {}
            for sp in range(NSP):
                hsp_evacs = []
                pending = []
                sdl_pre = {}
                for dc in range(2):
                    sdl_pre[dc] = load_wcol(wd[sp * SPW:(sp + 1) * SPW,
                                               dc * 128:(dc + 1) * 128])

                def emit_sparsify(ts, h_tok):
                    base = (ts - 1) * PBW
                    Hs = H[:, base:base + 2 * PBW]
                    Hr = Hs.rearrange("p (m r) -> p r m", r=4)
                    Ar = SPA[:, :].rearrange("p (m r) -> p r m", r=4)

                    def m(k):
                        return SPM[:, k * 256:(k + 1) * 256]

                    r.wait_tok("scalar", h_tok)
                    r.wait_tok("scalar", spa_free[0])
                    abs_tok = r.emit("scalar", lambda h, Hs=Hs:
                                     h.activation(out=SPA[:, :], in_=Hs,
                                                  func=AF.Abs), inc="sACT")
                    r.wait_tok("vector", abs_tok)
                    r.emit("vector", lambda h, Ar=Ar, m=m:
                           h.tensor_tensor(m(0), Ar[:, 0, :], Ar[:, 1, :], OP.max))
                    r.emit("vector", lambda h, Ar=Ar, m=m:
                           h.tensor_tensor(m(1), Ar[:, 0, :], Ar[:, 1, :], OP.min))
                    r.emit("vector", lambda h, Ar=Ar, m=m:
                           h.tensor_tensor(m(2), Ar[:, 2, :], Ar[:, 3, :], OP.max))
                    r.emit("vector", lambda h, Ar=Ar, m=m:
                           h.tensor_tensor(m(3), Ar[:, 2, :], Ar[:, 3, :], OP.min))
                    r.emit("vector", lambda h, m=m:
                           h.tensor_tensor(m(0), m(0), m(2), OP.min))
                    r.emit("vector", lambda h, m=m:
                           h.tensor_tensor(m(1), m(1), m(3), OP.max))
                    r.emit("vector", lambda h, m=m:
                           h.tensor_tensor(m(2), m(0), m(1), OP.max))
                    sp_tok = None
                    for rr in range(4):
                        r.emit("vector", lambda h, Ar=Ar, rr=rr, m=m:
                               h.tensor_tensor(m(3), Ar[:, rr, :], m(2),
                                               OP.is_ge))
                        sp_tok = r.emit("vector", lambda h, Hr=Hr, rr=rr, m=m:
                                        h.tensor_tensor(Hr[:, rr, :], Hr[:, rr, :],
                                                        m(3), OP.mult),
                                        inc="sDVE" if rr == 3 else None)
                    spa_free[0] = sp_tok
                    return sp_tok

                def emit_transpose(tpb, ts, sp_tok):
                    r.wait_tok("tensor", sp_tok)
                    half = ts % 2
                    pwait("tensor", ("ptr", half))
                    if down_read_tok is not None:
                        r.wait_tok("vector", down_read_tok)
                    PTR = pt["PTR"]
                    ttk = None
                    for k in range(4):
                        ttk = r.emit("tensor", lambda h, ts=ts, k=k, half=half, PTR=PTR:
                                     h.transpose(PTR[:, half * 1024 + k * 128:
                                                     half * 1024 + (k + 1) * 128],
                                                 H[:, ts * PBW + k * 128:
                                                   ts * PBW + (k + 1) * 128],
                                                 ID16[:, :]),
                                     inc="sPE" if k == 3 else None)
                    r.wait_tok("scalar", ttk)
                    c0 = tpb * 4
                    dstv = HSP[:].rearrange("p (c t) -> p c t", c=16)[
                        :, c0:c0 + 4, ts * 128:(ts + 1) * 128]
                    srcv = PTR[:, half * 1024:half * 1024 + 512].rearrange(
                        "p (k t) -> p k t", k=4)
                    ev = r.emit("scalar", lambda h, dstv=dstv, srcv=srcv:
                                h.copy(dstv, srcv), inc="sACT")
                    pfree[("ptr", half)] = ev
                    hsp_evacs.append(ev)

                for pb in range(NPB):
                    for ib in range(NIB):
                        bk = ((sp * NPB) + pb) * NIB + ib
                        issue_wgu_upto(2 * bk + 2)
                        sgate, gslot = load_toks[2 * bk]
                        sup, uslot = load_toks[2 * bk + 1]
                        r.wait(eng="tensor", key=sgate[0], val=sgate[1])
                        silu_toks = {}
                        ibpar = (pb * NIB + ib) % 2
                        SUb = SU if ibpar == 0 else SCB
                        gate_banks = [(("big", 0), big(0)), (("big", 1), big(1)),
                                      (("pc", 0), PC0)]
                        for ts in range(8):
                            r.wait_tok("tensor", xn_half[ts // 4])
                            gkey, gps = gate_banks[ts % 3]
                            pwait("tensor", gkey)
                            for c in range(NCH):
                                r.emit("tensor", lambda h, c=c, ts=ts, gps=gps, gslot=gslot:
                                       h.matmul(gps[:, 0:IBW],
                                                XNv(c, ts * 128, (ts + 1) * 128),
                                                WGU[:, gslot * NCH * IBW + c * IBW:
                                                    gslot * NCH * IBW + (c + 1) * IBW],
                                                start=(c == 0), stop=(c == NCH - 1)),
                                       inc="sPE" if c == NCH - 1 else None)
                            gt = r.tok("sPE")
                            if pending:
                                emit_transpose(*pending.pop(0))
                            r.wait_tok("scalar", gt)
                            r.wait_tok("scalar", su_free.get((ibpar, ts)))
                            stk = r.emit("scalar", lambda h, ts=ts, gps=gps, SUb=SUb:
                                         h.activation(out=SUb[:, ts * IBW:(ts + 1) * IBW],
                                                      in_=gps[:, 0:IBW],
                                                      func=AF.Silu), inc="sACT")
                            silu_toks[ts] = stk
                            pfree[gkey] = stk
                        wgu_slot_tok[gslot] = r.tok("sPE")
                        if ib == 0:
                            h_free = r.tok("sPE")
                        r.wait(eng="tensor", key=sup[0], val=sup[1])
                        last_ib = ib == NIB - 1
                        up_banks = [(("big", 2), big(2)), (("big", 3), big(3)),
                                    (("pc", 1), PC1)]
                        for ts in range(8):
                            ukey, ups = up_banks[ts % 3]
                            pwait("tensor", ukey)
                            for c in range(NCH):
                                r.emit("tensor", lambda h, c=c, ts=ts, ups=ups, uslot=uslot:
                                       h.matmul(ups[:, 0:IBW],
                                                XNv(c, ts * 128, (ts + 1) * 128),
                                                WGU[:, uslot * NCH * IBW + c * IBW:
                                                    uslot * NCH * IBW + (c + 1) * IBW],
                                                start=(c == 0), stop=(c == NCH - 1)),
                                       inc="sPE" if c == NCH - 1 else None)
                            ut = r.tok("sPE")
                            r.wait_tok("vector", ut)
                            r.wait_tok("vector", silu_toks[ts])
                            if h_free is not None:
                                r.wait_tok("vector", h_free)
                            htk = r.emit("vector", lambda h, ts=ts, ib=ib, ups=ups, SUb=SUb:
                                         h.tensor_tensor(
                                             H[:, ts * PBW + ib * IBW:
                                               ts * PBW + (ib + 1) * IBW],
                                             SUb[:, ts * IBW:(ts + 1) * IBW],
                                             ups[:, 0:IBW], OP.mult), inc="sDVE")
                            su_free[(ibpar, ts)] = htk
                            pfree[ukey] = htk
                            if last_ib and ts % 2 == 1:
                                ptok = emit_sparsify(ts, htk)
                                pending.append((pb, ts - 1, ptok))
                                pending.append((pb, ts, ptok))
                        wgu_slot_tok[uslot] = r.tok("sPE")
                    if pb == NPB - 1:
                        while pending:
                            emit_transpose(*pending.pop(0))
                        h_free = r.tok("sPE")
                hsp_done = hsp_evacs[-1]
                for dc in range(NCH):
                    if dc == 8:
                        issue_wgu_upto(2 * ((sp + 1) * NPB * NIB) + 2)
                    if dc < 2:
                        sdl = sdl_pre[dc]
                    else:
                        sdl = load_wcol(wd[sp * SPW:(sp + 1) * SPW,
                                           dc * 128:(dc + 1) * 128])
                    cslot = sdl[2]
                    r.wait(eng="tensor", key=sdl[0], val=sdl[1])
                    r.wait_tok("tensor", hsp_done)
                    down_banks = [(("pc", 0), PC0), (("pc", 1), PC1),
                                  (("big", 0), big(0)), (("big", 1), big(1))]
                    for th in range(2):
                        dkey, pc = down_banks[(dc * 2 + th) % 4]
                        pwait("tensor", dkey)
                        for c in range(NCH):
                            r.emit("tensor", lambda h, pc=pc, c=c, th=th, cslot=cslot:
                                   h.matmul(pc[:, :],
                                            WCOL[:, cslot * 2048 + c * 128:
                                                 cslot * 2048 + (c + 1) * 128],
                                            HSPv(c, th * 512, (th + 1) * 512),
                                            start=(c == 0), stop=(c == NCH - 1)),
                                   inc="sPE" if c == NCH - 1 else None)
                        dt = r.tok("sPE")
                        r.wait_tok("vector", dt)
                        stk = r.emit("vector", lambda h, pc=pc, dc=dc, th=th:
                                     h.scalar_tensor_tensor(
                                         Xv(dc, th * 512, (th + 1) * 512), pc[:, :],
                                         0.5, Xv(dc, th * 512, (th + 1) * 512),
                                         OP.mult, OP.add), inc="sDVE")
                        pfree[dkey] = stk
                        stt_toks[(dc, th)] = stk
                    wcol_slot_tok[cslot] = r.tok("sPE")
                down_read_tok = r.tok("sPE")
            x_tok_map = {}
            for c in range(NCH):
                a, b = stt_toks[(c, 0)], stt_toks[(c, 1)]
                x_tok_map[c] = b if b[1] >= a[1] else a
            return x_tok_map

        def attention(xn_half):
            pq_free = {}
            scf_free = None
            pq2_gate = [None]
            qb_gate = {0: None, 1: None}

            def qkv_chunk(oc, dest_kind, dest_slot):
                nonlocal pq_free, scf_free
                sload = load_wcol(wqkv[:, oc * 128:(oc + 1) * 128])
                cslot = sload[2]
                r.wait(eng="tensor", key=sload[0], val=sload[1])
                last = None
                for th in range(2):
                    PQb = pt["PQ"] if (th == 0 or "PQ2" not in pt) else pt["PQ2"]
                    key = "PQ" if PQb is pt["PQ"] else "PQ2"
                    r.wait_tok("tensor", xn_half[th])
                    for tk in pq_free.get(key, []):
                        r.wait_tok("tensor", tk)
                    if key == "PQ2" and pq2_gate[0] is not None:
                        r.wait_tok("tensor", pq2_gate[0])
                        pq2_gate[0] = None
                    for c in range(NCH):
                        r.emit("tensor", lambda h, c=c, th=th, cslot=cslot, PQb=PQb:
                               h.matmul(PQb[:, :],
                                        WCOL[:, cslot * 2048 + c * 128:
                                             cslot * 2048 + (c + 1) * 128],
                                        XNv(c, th * 512, (th + 1) * 512),
                                        start=(c == 0), stop=(c == NCH - 1)),
                               inc="sPE" if c == NCH - 1 else None)
                    qt = r.tok("sPE")
                    if dest_kind == "v":
                        r.wait_tok("scalar", qt)
                        r.wait_tok("scalar", scf_free)
                        vct = r.emit("scalar", lambda h, PQb=PQb:
                                     h.activation(out=SCF[:, 0:512], in_=PQb[:, :],
                                                  func=AF.Copy), inc="sACT")
                        pq_free[key] = [vct]
                        r.wait_tok("tensor", vct)
                        pwait("tensor", ("ptr", 0))
                        vtt = None
                        PTRV = pt["PTRV"]
                        for k in range(4):
                            vtt = r.emit("tensor", lambda h, k=k, PTRV=PTRV:
                                         h.transpose(PTRV[:, k * 128:(k + 1) * 128],
                                                     SCF[:, k * 128:(k + 1) * 128],
                                                     ID16[:, :]),
                                         inc="sPE" if k == 3 else None)
                        r.wait_tok("vector", vtt)
                        last = r.emit("vector", lambda h, dest_slot=dest_slot, th=th:
                                      h.tensor_copy(
                                          SCB[:, dest_slot * 1024 + th * 512:
                                              dest_slot * 1024 + (th + 1) * 512],
                                          pt["PTRV"][:, 0:512]), inc="sDVE")
                        pfree[("ptr", 0)] = last
                        scf_free = last
                    elif "rot" not in ATTN_PARTS:
                        r.wait_tok("scalar", qt)
                        last = r.emit("scalar", lambda h, dest_slot=dest_slot, th=th:
                                      h.activation(out=SCB[:, dest_slot * 1024 + th * 512:
                                                       dest_slot * 1024 + (th + 1) * 512],
                                                   in_=pt["PQ"][:, :], func=AF.Copy),
                                      inc="sACT")
                        pq_free = [last]
                    else:
                        r.wait_tok("scalar", qt)
                        r.wait_tok("scalar", scf_free)
                        r.emit("scalar", lambda h, PQb=PQb:
                               h.copy(SCF[0:64, 512:1024], PQb[64:128, :]))
                        swt = r.emit("scalar", lambda h, PQb=PQb:
                                     h.copy(SCF[64:128, 512:1024], PQb[0:64, :]),
                                     inc="sACT")
                        r.wait_tok("vector", qt)
                        r.wait_tok("vector", swt)
                        r.emit("vector", lambda h, th=th, PQb=PQb:
                               h.tensor_tensor(SCF[:, 0:512], PQb[:, :],
                                               TBA[:, th * 512:(th + 1) * 512],
                                               OP.mult))
                        r.emit("vector", lambda h, th=th:
                               h.tensor_tensor(SCF[:, 512:1024], SCF[:, 512:1024],
                                               TBB[:, th * 512:(th + 1) * 512],
                                               OP.mult))
                        if dest_kind == "k":
                            dv = SCB[:, dest_slot * 1024 + th * 512:
                                     dest_slot * 1024 + (th + 1) * 512]
                        else:
                            r.wait_tok("vector", qb_gate[dest_slot % 2])
                            dv = QB[:, (dest_slot % 2) * 1024 + th * 512:
                                    (dest_slot % 2) * 1024 + (th + 1) * 512]
                        last = r.emit("vector", lambda h, dv=dv:
                                      h.tensor_tensor(dv, SCF[:, 0:512],
                                                      SCF[:, 512:1024], OP.add),
                                      inc="sDVE")
                        pq_free[key] = [last]
                        scf_free = last
                wcol_slot_tok[cslot] = r.tok("sPE")
                return last

            ptrv_stack = contextlib.ExitStack()
            pt["PTRV"] = ptrv_stack.enter_context(
                nc.psum_tensor("PTRVa", [128, 1024], F16))
            kv_send = []
            for kv in range(2):
                if "k%d" % kv not in ATTN_PARTS:
                    continue
                ktok = qkv_chunk(16 + kv, "k", kv)
                if "send" not in ATTN_PARTS:
                    continue
                r.wait_tok("sync", ktok)
                kv_send.append(r.emit("sync", lambda h, kv=kv:
                                      h.dma_start(out=cc_send[kv * 128:(kv + 1) * 128, :],
                                                  in_=SCB[:, kv * 1024:(kv + 1) * 1024]),
                                      inc="sIO", n=16))
            for kv in range(2):
                if "v%d" % kv not in ATTN_PARTS:
                    continue
                if "send" in ATTN_PARTS:
                    r.wait_tok("vector", kv_send[kv])
                vtok = qkv_chunk(18 + kv, "v", kv)
                if "send" not in ATTN_PARTS:
                    continue
                r.wait_tok("sync", vtok)
                kv_send.append(r.emit("sync", lambda h, kv=kv:
                                      h.dma_start(
                                          out=cc_send[256 + kv * 128:
                                                      256 + (kv + 1) * 128, :],
                                          in_=SCB[:, kv * 1024:(kv + 1) * 1024]),
                                      inc="sIO", n=16))
            if "cc" not in ATTN_PARTS:
                return None
            ptrv_stack.close()
            pq2_stack = contextlib.ExitStack()
            pt["PQ2"] = pq2_stack.enter_context(
                nc.psum_tensor("PQ2a", [128, 512], F32))
            pq2_gate[0] = r.tok("sDVE")
            r.wait(eng="gpsimd", key="sIO", val=r.cnt["sIO"])
            r.emit("gpsimd", lambda h:
                   h.collective_compute("AllGather", OP.bypass,
                                        replica_groups=[[0, 1], [2, 3], [4, 5], [6, 7]],
                                        ins=[cc_send.ap().opt()],
                                        outs=[cc_recv.ap().opt()]), inc="sCC")

            def load_kv(kv, gate_tok):
                r.wait(eng="sync", key="sCC", val=1)
                r.wait_tok("sync", gate_tok)
                for dst, row in ((KG[:, 0:1024], kv * 128),
                                 (KG[:, 1024:2048], 512 + kv * 128),
                                 (VT[:, 0:1024], 256 + kv * 128),
                                 (VT[:, 1024:2048], 768 + kv * 128)):
                    r.emit("sync", lambda h, dst=dst, row=row:
                           h.dma_start(out=dst, in_=cc_recv[row:row + 128, :]),
                           inc="sIO", n=16)
                return r.tok("sIO")

            kv_tok = load_kv(0, None)
            if "q" not in ATTN_PARTS:
                return None
            q_toks = {0: qkv_chunk(0, "q", 0), 1: qkv_chunk(1, "q", 1)}
            dn_o_free = None
            a_last = None
            head_pass_end = {}

            if "heads" not in ATTN_PARTS:
                return None
            for hq in range(NH):
                if hq == 8:
                    kv_tok = load_kv(1, head_pass_end[7])
                r.wait(eng="tensor", key=kv_tok[0], val=kv_tok[1])
                r.wait_tok("tensor", q_toks[hq])
                ebf_free = {0: None, 1: None}
                for b in range(2):
                    tiles = []
                    for mm in range(4 * b + 4):
                        for pi in range(2):
                            N = 512 if mm < 4 * b else 512 - 128 * (mm - 4 * b)
                            tiles.append((pi, mm, N))
                    batches = []
                    cur, off = [], 0
                    for (pi, mm, N) in tiles:
                        noff = off
                        if noff % 512 + N > 512:
                            noff = (noff // 512 + 1) * 512
                        if noff + N > 1024:
                            batches.append(cur)
                            cur, noff = [], 0
                        cur.append((pi, mm, N, noff))
                        off = noff + N
                    if cur:
                        batches.append(cur)
                    first = [True]
                    pend = []

                    def emit_dnpv(batch, eslot, ext, mask_tok, last_batch):
                        r.wait_tok("tensor", ext)
                        if mask_tok is not None:
                            r.wait_tok("tensor", mask_tok)
                        if first[0]:
                            pwait("tensor", ("pc", 0), ("pc", 1))
                            if dn_o_free is not None:
                                r.wait_tok("tensor", dn_o_free)
                        pvt = None
                        for j, (pi, mm, N, off) in enumerate(batch):
                            is_first = first[0] and j == 0
                            is_last = last_batch and j == len(batch) - 1
                            r.emit("tensor",
                                   lambda h, N=N, off=off, eslot=eslot,
                                   is_first=is_first, is_last=is_last:
                                   h.matmul(PC0[:, 512 - N:512], ONES[:, :],
                                            EBF[:, eslot * 1024 + off:
                                                eslot * 1024 + off + N],
                                            start=is_first, stop=is_last))
                            pvt = r.emit("tensor",
                                         lambda h, pi=pi, mm=mm, N=N, off=off,
                                         eslot=eslot, is_first=is_first,
                                         is_last=is_last:
                                         h.matmul(PC1[:, 512 - N:512],
                                                  VT[:, (pi * 8 + mm) * 128:
                                                     (pi * 8 + mm + 1) * 128],
                                                  EBF[:, eslot * 1024 + off:
                                                      eslot * 1024 + off + N],
                                                  start=is_first, stop=is_last),
                                         inc="sPE" if j == len(batch) - 1 else None)
                        ebf_free[eslot] = pvt
                        first[0] = False

                    for bi, batch in enumerate(batches):
                        sb = bi % 2
                        eslot = bi % 2
                        pwait("tensor", ("big", 2 * sb), ("big", 2 * sb + 1))
                        W = max(off2 + N2 for (pi2, mm2, N2, off2) in batch)
                        spt = None
                        for j, (pi, mm, N, off) in enumerate(batch):
                            qlo = b * 512 + (512 - N)
                            spt = r.emit(
                                "tensor",
                                lambda h, pi=pi, mm=mm, N=N, off=off, sb=sb, hq=hq,
                                qlo=qlo:
                                h.matmul(bighalf(sb)[:, off:off + N],
                                         KG[:, pi * 1024 + mm * 128:
                                            pi * 1024 + (mm + 1) * 128],
                                         QB[:, (hq % 2) * 1024 + qlo:
                                            (hq % 2) * 1024 + qlo + N],
                                         start=True, stop=True),
                                inc="sPE" if j == len(batch) - 1 else None)
                        r.wait_tok("scalar", spt)
                        r.wait_tok("scalar", ebf_free[eslot])
                        r.emit("scalar", lambda h, sb=sb, W=W:
                               h.activation(out=ESB[:, 0:W], in_=bighalf(sb)[:, 0:W],
                                            func=AF.Tanh, scale=SCALE / CAP))
                        ext = r.emit("scalar", lambda h, eslot=eslot, W=W:
                                     h.activation(
                                         out=EBF[:, eslot * 1024:eslot * 1024 + W],
                                         in_=ESB[:, 0:W], func=AF.Exp, scale=CAP),
                                     inc="sACT")
                        pfree[("big", 2 * sb)] = ext
                        pfree[("big", 2 * sb + 1)] = ext
                        mask_tok = None
                        diag = [t for t in batch if t[1] >= 4 * b]
                        if diag:
                            r.wait_tok("vector", ext)
                            for j, (pi, mm, N, off) in enumerate(diag):
                                mask_tok = r.emit(
                                    "vector",
                                    lambda h, eslot=eslot, off=off, pi=pi:
                                    h.tensor_tensor(
                                        EBF[:, eslot * 1024 + off:
                                            eslot * 1024 + off + 128],
                                        EBF[:, eslot * 1024 + off:
                                            eslot * 1024 + off + 128],
                                        (MP0 if pi == 0 else MP1)[:, :], OP.mult),
                                    inc="sDVE" if j == len(diag) - 1 else None)
                        pend.append((batch, eslot, ext, mask_tok,
                                     bi == len(batches) - 1))
                        if len(pend) > 1:
                            emit_dnpv(*pend.pop(0))
                    while pend:
                        emit_dnpv(*pend.pop(0))
                    r.wait_tok("vector", r.tok("sPE"))
                    r.emit("vector", lambda h: h.reciprocal(XSTG[:, 0:512], PC0[:, :]))
                    aev = r.emit("vector", lambda h, hq=hq, b=b:
                                 h.tensor_tensor(HSPv(hq, b * 512, (b + 1) * 512),
                                                 PC1[:, :], XSTG[:, 0:512], OP.mult),
                                 inc="sDVE")
                    dn_o_free = aev
                    a_last = aev
                head_pass_end[hq] = r.tok("sPE")
                qb_gate[hq % 2] = head_pass_end[hq]
                if hq + 2 < NH:
                    q_toks[hq + 2] = qkv_chunk(hq + 2, "q", hq + 2)
            if "wo" not in ATTN_PARTS:
                return None
            stt_toks = {}
            for dc in range(NCH):
                sload = load_wcol(wo[:, dc * 128:(dc + 1) * 128])
                cslot = sload[2]
                r.wait(eng="tensor", key=sload[0], val=sload[1])
                r.wait_tok("tensor", a_last)
                if dn_o_free is not None:
                    r.wait_tok("tensor", dn_o_free)
                wo_banks = [(("pc", 0), PC0), (("pc", 1), PC1),
                            (("big", 0), big(0)), (("big", 1), big(1))]
                for th in range(2):
                    wkey, pc = wo_banks[(dc * 2 + th) % 4]
                    pwait("tensor", wkey)
                    for c in range(NCH):
                        r.emit("tensor", lambda h, pc=pc, c=c, th=th, cslot=cslot:
                               h.matmul(pc[:, :],
                                        WCOL[:, cslot * 2048 + c * 128:
                                             cslot * 2048 + (c + 1) * 128],
                                        HSPv(c, th * 512, (th + 1) * 512),
                                        start=(c == 0), stop=(c == NCH - 1)),
                               inc="sPE" if c == NCH - 1 else None)
                    wt = r.tok("sPE")
                    r.wait_tok("vector", wt)
                    stk = r.emit("vector", lambda h, pc=pc, dc=dc, th=th:
                                 h.scalar_tensor_tensor(
                                     Xv(dc, th * 512, (th + 1) * 512), pc[:, :], 1.0,
                                     Xv(dc, th * 512, (th + 1) * 512),
                                     OP.mult, OP.add), inc="sDVE")
                    pfree[wkey] = stk
                    stt_toks[(dc, th)] = stk
                wcol_slot_tok[cslot] = r.tok("sPE")
            x_tok_map = {}
            for c in range(NCH):
                a, b = stt_toks[(c, 0)], stt_toks[(c, 1)]
                x_tok_map[c] = b if b[1] >= a[1] else a
            pq2_stack.close()
            pt.pop("PQ2", None)
            return x_tok_map

        x_map = x_ready
        if "ffn1" in PHASES:
            xa = layernorm(x_map)
            with nc.psum_tensor("PTRa", [128, 2048], F16) as _p:
                pt["PTR"] = _p
                x_map = ffn(xa)
        if "attn" in PHASES:
            xa = layernorm(x_map)
            with nc.psum_tensor("PQa", [128, 512], F32) as _pq:
                pt["PQ"] = _pq
                xm2 = attention(xa)
            if xm2 is not None:
                x_map = xm2
        if "ffn2" in PHASES:
            xa = layernorm(x_map)
            with nc.psum_tensor("PTRb", [128, 2048], F16) as _p:
                pt["PTR"] = _p
                x_map = ffn(xa)

        stg_free = {0: None, 1: None}
        for ts in range(8):
            for dh in range(2):
                for k4 in range(2):
                    half = (ts * 4 + dh * 2 + k4) % 2
                    q = k4
                    pwait("tensor", ("big", q))
                    tk = None
                    for k in range(4):
                        c = dh * 8 + k4 * 4 + k
                        r.wait_tok("tensor", x_map.get(c))
                        tk = r.emit("tensor", lambda h, c=c, ts=ts, q=q, k=k:
                                    h.transpose(big(q)[:, k * 128:(k + 1) * 128],
                                                Xv(c, ts * 128, (ts + 1) * 128),
                                                ID32[:, :]),
                                    inc="sPE" if k == 3 else None)
                    r.wait_tok("vector", tk)
                    r.wait_tok("vector", stg_free[half])
                    ev = r.emit("vector", lambda h, half=half, q=q:
                                h.tensor_copy(XSTG[:, half * 512:(half + 1) * 512],
                                              big(q)), inc="sDVE")
                    pfree[("big", q)] = ev
                    r.wait_tok("sync", ev)
                    dt = r.emit("sync", lambda h, ts=ts, dh=dh, k4=k4, half=half:
                                h.dma_start(out=out[ts * 128:(ts + 1) * 128,
                                                    dh * 1024 + k4 * 512:
                                                    dh * 1024 + (k4 + 1) * 512],
                                            in_=XSTG[:, half * 512:(half + 1) * 512]),
                                inc="sIO", n=16)
                    stg_free[half] = dt
        r.wait(eng="sync", key="sIO", val=r.cnt["sIO"])

        def make_body(ename):
            ops = r.ops[ename]

            def body(h):
                for op in ops:
                    if op[0] == "w":
                        h.wait_ge(sems[op[1]], op[2])
                    else:
                        inst = op[1](h)
                        if op[2] is not None:
                            inst.then_inc(sems[op[2]], op[3])
            return body

        blk.tensor(make_body("tensor"))
        blk.vector(make_body("vector"))
        blk.scalar(make_body("scalar"))
        blk.gpsimd(make_body("gpsimd"))
        blk.sync(make_body("sync"))

    return nc



def _host_tables(parity):
    inv_freq = 1.0 / (10000.0 ** (np.arange(0, HD, 2, dtype=np.float64) / HD))
    pos = 2.0 * np.arange(NT, dtype=np.float64) + parity
    fr = np.outer(inv_freq, pos)
    c = np.cos(fr)
    s = np.sin(fr)
    tbl_a = np.concatenate([c, c], axis=0).astype(np.float16)
    tbl_b = np.concatenate([-s, s], axis=0).astype(np.float16)
    return tbl_a, tbl_b


def kernel(L0_attn_hidden_states, L0_attn_Wqkv, L0_attn_Wo,
           L0_ffn_w_gate, L0_ffn_w_up, L0_ffn_w_down):
    x = np.asarray(L0_attn_hidden_states)
    wqkv = np.ascontiguousarray(np.asarray(L0_attn_Wqkv), dtype=np.float32)
    wov = np.ascontiguousarray(np.asarray(L0_attn_Wo), dtype=np.float32)
    wgt = np.ascontiguousarray(np.asarray(L0_ffn_w_gate), dtype=np.float32)
    wup = np.ascontiguousarray(np.asarray(L0_ffn_w_up), dtype=np.float32)
    wdn = np.ascontiguousarray(np.asarray(L0_ffn_w_down), dtype=np.float32)

    jj = np.arange(128)[:, None]
    ii = np.arange(128)[None, :]
    mp0 = (jj <= ii).astype(ml_dtypes.bfloat16)
    mp1_even = (jj < ii).astype(ml_dtypes.bfloat16)
    mp1_odd = (jj <= ii).astype(ml_dtypes.bfloat16)
    id32 = np.eye(128, dtype=np.float32)
    id16 = np.eye(128, dtype=np.float16)
    ones_bf = np.ones((128, 128), dtype=ml_dtypes.bfloat16)

    nc = build_program()
    in_maps = []
    for core in range(8):
        b = core // 2
        parity = core % 2
        tbl_a, tbl_b = _host_tables(parity)
        in_maps.append({
            "x": np.ascontiguousarray(x[b, parity::2, :], dtype=np.float32),
            "wqkv": wqkv, "wo": wov, "wg": wgt, "wu": wup, "wd": wdn,
            "tbl_a": tbl_a, "tbl_b": tbl_b,
            "mask_p0": mp0, "mask_p1": (mp1_even if parity == 0 else mp1_odd),
            "ident32": id32, "ident16": id16, "ones_bf": ones_bf,
            "eps32": np.full((128, 1), LN_EPS, dtype=np.float32),
        })
    res = run_bass_kernel_spmd(nc, in_maps, core_ids=list(range(8)))
    outp = np.empty((4, 2048, 2048), dtype=np.float32)
    for core in range(8):
        b = core // 2
        parity = core % 2
        outp[b, parity::2, :] = res.results[core]["out"]
    return outp


# version 19 (final)
# speedup vs baseline: 1.0832x; 1.0342x over previous
import numpy as np
import ml_dtypes
import contextlib
import concourse.bass as bass
import concourse.mybir as mybir
from concourse.bass_utils import run_bass_kernel_spmd
from concourse.alu_op_type import AluOpType as OP

F32 = mybir.dt.float32
BF16 = mybir.dt.bfloat16
F16 = mybir.dt.float16
AF = mybir.ActivationFunctionType

D = 2048
NCH = 16
NT = 1024
NH = 16
HD = 128
II = 8192
SCALE = 0.08838834764831845
CAP = 30.0
LN_EPS = 1e-5
QKV_OUT = 2560

IBW = 256
PBW = 512
SPW = 2048
NSP = II // SPW
NPB = SPW // PBW
NIB = PBW // IBW

PHASES = ("ffn1", "attn", "ffn2")
ATTN_PARTS = ("k0", "k1", "v0", "v1", "send", "cc", "q", "heads", "wo", "rot")

ENGS = ("tensor", "vector", "scalar", "gpsimd", "sync")


class Rec:

    def __init__(self, sems):
        self.ops = {e: [] for e in ENGS}
        self.sems = sems
        self.cnt = {k: 0 for k in sems}
        self.last_wait = {}

    def wait(self, eng, key, val):
        if val is None or val <= 0:
            return
        if self.last_wait.get((eng, key), 0) >= val:
            return
        self.last_wait[(eng, key)] = val
        self.ops[eng].append(("w", key, val))

    def wait_tok(self, eng, tok):
        if tok is not None:
            self.wait(eng, tok[0], tok[1])

    def emit(self, eng, fn, inc=None, n=1):
        self.ops[eng].append(("c", fn, inc, n))
        if inc is not None:
            self.cnt[inc] += n
            return (inc, self.cnt[inc])
        return None

    def tok(self, key):
        return (key, self.cnt[key])


def build_program():
    nc = bass.Bass("TRN2", target_bir_lowering=False)

    x_in = nc.dram_tensor("x", [NT, D], F32, kind="ExternalInput").ap()
    wqkv = nc.dram_tensor("wqkv", [D, QKV_OUT], F32, kind="ExternalInput").ap()
    wo = nc.dram_tensor("wo", [D, D], F32, kind="ExternalInput").ap()
    wg = nc.dram_tensor("wg", [D, II], F32, kind="ExternalInput").ap()
    wu = nc.dram_tensor("wu", [D, II], F32, kind="ExternalInput").ap()
    wd = nc.dram_tensor("wd", [II, D], F32, kind="ExternalInput").ap()
    tbl_a_in = nc.dram_tensor("tbl_a", [128, NT], F16, kind="ExternalInput").ap()
    tbl_b_in = nc.dram_tensor("tbl_b", [128, NT], F16, kind="ExternalInput").ap()
    mask_p0_in = nc.dram_tensor("mask_p0", [128, 128], BF16, kind="ExternalInput").ap()
    mask_p1_in = nc.dram_tensor("mask_p1", [128, 128], BF16, kind="ExternalInput").ap()
    ident32_in = nc.dram_tensor("ident32", [128, 128], F32, kind="ExternalInput").ap()
    ident16_in = nc.dram_tensor("ident16", [128, 128], F16, kind="ExternalInput").ap()
    ones_in = nc.dram_tensor("ones_bf", [128, 128], BF16, kind="ExternalInput").ap()
    eps_in = nc.dram_tensor("eps32", [128, 1], F32, kind="ExternalInput").ap()
    out = nc.dram_tensor("out", [NT, D], F32, kind="ExternalOutput").ap()
    cc_send = nc.dram_tensor("cc_send", [512, NT], BF16)
    cc_recv = nc.dram_tensor("cc_recv", [1024, NT], BF16)

    st = contextlib.ExitStack()
    with st:
        X = st.enter_context(nc.sbuf_tensor("Xst", [128, NCH * NT], F32))
        XN = st.enter_context(nc.sbuf_tensor("XNst", [128, NCH * NT], BF16))
        HSP = st.enter_context(nc.sbuf_tensor("HSPst", [128, 16 * NT], BF16))
        H = st.enter_context(nc.sbuf_tensor("Hst", [128, 8 * PBW], F16))
        WGU = st.enter_context(nc.sbuf_tensor("WGUst", [128, 2 * NCH * IBW], BF16))
        WCOL = st.enter_context(nc.sbuf_tensor("WCOLst", [128, 2 * 2048], BF16))
        SU = st.enter_context(nc.sbuf_tensor("SUst", [128, 8 * IBW], F16))
        SPA = st.enter_context(nc.sbuf_tensor("SPAst", [128, 2 * PBW], F16))
        SPM = st.enter_context(nc.sbuf_tensor("SPMst", [128, 4 * 256], F16))
        ESB = st.enter_context(nc.sbuf_tensor("ESBst", [128, 1024], F16))
        EBF = st.enter_context(nc.sbuf_tensor("EBFst", [128, 2 * 1024], BF16))
        KG = st.enter_context(nc.sbuf_tensor("KGst", [128, 2048], BF16))
        VT = st.enter_context(nc.sbuf_tensor("VTst", [128, 2048], BF16))
        TBA = st.enter_context(nc.sbuf_tensor("TBAst", [128, NT], F16))
        TBB = st.enter_context(nc.sbuf_tensor("TBBst", [128, NT], F16))
        MU = st.enter_context(nc.sbuf_tensor("MUst", [128, 512], BF16))
        RS = st.enter_context(nc.sbuf_tensor("RSst", [128, 512], F32))
        SCF = st.enter_context(nc.sbuf_tensor("SCFst", [128, 2 * 512], F16))
        SCB = st.enter_context(nc.sbuf_tensor("SCBst", [128, 2 * 1024], BF16))
        QB = st.enter_context(nc.sbuf_tensor("QBst", [128, 2 * 1024], BF16))
        XSTG = st.enter_context(nc.sbuf_tensor("XSTGst", [128, 1024], F32))
        ID32 = st.enter_context(nc.sbuf_tensor("ID32st", [128, 128], F32))
        ID16 = st.enter_context(nc.sbuf_tensor("ID16st", [128, 128], F16))
        ONES = st.enter_context(nc.sbuf_tensor("ONESst", [128, 128], BF16))
        MP0 = st.enter_context(nc.sbuf_tensor("MP0st", [128, 128], BF16))
        MP1 = st.enter_context(nc.sbuf_tensor("MP1st", [128, 128], BF16))
        EPS = st.enter_context(nc.sbuf_tensor("EPSst", [128, 1], F32))

        PBIG = st.enter_context(nc.psum_tensor("PBIG", [128, 2048], F32))
        PC0 = st.enter_context(nc.psum_tensor("PC0", [128, 512], F32))
        PC1 = st.enter_context(nc.psum_tensor("PC1", [128, 512], F32))
        pt = {}

        keys = ["sW0", "sW1", "sC0", "sC1", "sC2", "sC3", "sX0", "sX1", "sIO",
                "sCC", "sPE", "sACT", "sDVE"]
        sems = {}
        for k in keys:
            sems[k] = st.enter_context(nc.semaphore(k))
        blk = st.enter_context(nc.Block())

        r = Rec(sems)
        pfree = {}

        def pwait(eng, *regions):
            for reg in regions:
                r.wait_tok(eng, pfree.get(reg))

        def Xv(c, a=0, b=NT):
            return X[:, c * NT + a: c * NT + b]

        def XNv(c, a=0, b=NT):
            return XN[:, c * NT + a: c * NT + b]

        def HSPv(c, a=0, b=NT):
            return HSP[:, c * NT + a: c * NT + b]

        def big(q):
            return PBIG[:, q * 512:(q + 1) * 512]

        def bighalf(j):
            return PBIG[:, j * 1024:(j + 1) * 1024]

        for dst, src in ((TBA, tbl_a_in), (TBB, tbl_b_in), (MP0, mask_p0_in),
                         (MP1, mask_p1_in), (ID32, ident32_in), (ID16, ident16_in),
                         (ONES, ones_in), (EPS, eps_in)):
            r.emit("sync", lambda h, d=dst, s=src: h.dma_start(out=d[:, :], in_=s),
                   inc="sIO", n=16)
        consts_tok = r.tok("sIO")
        for e in ("tensor", "vector", "scalar"):
            r.wait_tok(e, consts_tok)

        xin_evac = {}
        xstg_free = {0: None, 1: None}
        nx = {0: 0, 1: 0}
        it = [0]
        for dh in range(2):
            for k4 in range(2):
                for ts in range(8):
                    half = it[0] % 2
                    it[0] += 1
                    skey = "sX%d" % half
                    stg = XSTG[:, half * 512:(half + 1) * 512]
                    r.wait_tok("sync", xstg_free[half])
                    nx[half] += 1
                    r.emit("sync", lambda h, ts=ts, dh=dh, k4=k4, stg=stg:
                           h.dma_start(out=stg,
                                       in_=x_in[ts * 128:(ts + 1) * 128,
                                                dh * 1024 + k4 * 512:
                                                dh * 1024 + (k4 + 1) * 512]),
                           inc=skey, n=16)
                    r.wait(eng="tensor", key=skey, val=nx[half] * 16)
                    q = ts % 2
                    pwait("tensor", ("big", q))
                    t_pe = None
                    for k in range(4):
                        t_pe = r.emit("tensor", lambda h, q=q, k=k, stg=stg:
                                      h.transpose(big(q)[:, k * 128:(k + 1) * 128],
                                                  stg[:, k * 128:(k + 1) * 128],
                                                  ID32[:, :]),
                                      inc="sPE" if k == 3 else None)
                    xstg_free[half] = r.tok("sPE")
                    r.wait_tok("scalar", t_pe)
                    c0 = dh * 8 + k4 * 4
                    dstv = X[:].rearrange("p (c t) -> p c t", c=NCH)[
                        :, c0:c0 + 4, ts * 128:(ts + 1) * 128]
                    srcv = big(q).rearrange("p (k t) -> p k t", k=4)
                    ev = r.emit("scalar", lambda h, dstv=dstv, srcv=srcv:
                                h.copy(dstv, srcv), inc="sACT")
                    pfree[("big", q)] = ev
                    for c in range(c0, c0 + 4):
                        xin_evac[c] = ev

        x_ready = dict(xin_evac)

        wgu_slot_tok = {0: None, 1: None}
        wgu_loads = {0: 0, 1: 0}
        wcol_slot_tok = {0: None, 1: None}
        wcol_loads = {0: 0, 1: 0}
        wcol_ctr = [0]

        def load_wgu(src_ap, slot):
            skey = "sW%d" % slot
            r.wait_tok("gpsimd", wgu_slot_tok[slot])
            dst = WGU[:, slot * NCH * IBW:(slot + 1) * NCH * IBW].rearrange(
                "p (c i) -> p c i", c=NCH)
            src = src_ap.rearrange("(c p) i -> p c i", p=128)
            r.emit("gpsimd", lambda h, dst=dst, src=src: h.dma_start(out=dst, in_=src),
                   inc=skey, n=16)
            wgu_loads[slot] += 1
            return (skey, wgu_loads[slot] * 16)

        def load_wcol(src_ap, slot=None, split=False):
            slot = wcol_ctr[0] % 2
            wcol_ctr[0] += 1
            r.wait_tok("gpsimd", wcol_slot_tok[slot])
            if not split:
                skey = "sC%d" % slot
                dst = WCOL[:, slot * 2048:(slot + 1) * 2048].rearrange(
                    "p (c i) -> p c i", c=NCH)
                src = src_ap.rearrange("(c p) i -> p c i", p=128)
                r.emit("gpsimd", lambda h, dst=dst, src=src:
                       h.dma_start(out=dst, in_=src), inc=skey, n=16)
                wcol_loads[slot] += 1
                return (skey, wcol_loads[slot] * 16, slot, None)
            toks = []
            for hh in range(2):
                skey = "sC%d" % (slot * 2 + hh)
                dst = WCOL[:, slot * 2048 + hh * 1024:
                           slot * 2048 + (hh + 1) * 1024].rearrange(
                    "p (c i) -> p c i", c=NCH // 2)
                srcv = src_ap[hh * 1024:(hh + 1) * 1024, :].rearrange(
                    "(c p) i -> p c i", p=128)
                r.emit("gpsimd", lambda h, dst=dst, srcv=srcv:
                       h.dma_start(out=dst, in_=srcv), inc=skey, n=16)
                wcol_loads[slot * 2 + hh] = wcol_loads.get(slot * 2 + hh, 0) + 1
                toks.append((skey, wcol_loads[slot * 2 + hh] * 16))
            return (toks[0][0], toks[0][1], slot, toks[1])

        def layernorm(x_tok_map):
            for q in range(4):
                pwait("tensor", ("big", q))
            stats_tok = {}
            xn_half = {}
            for c in range(NCH):
                buf = SCB if c % 2 == 0 else QB
                r.wait_tok("scalar", x_tok_map.get(c))
                if c >= 2:
                    r.wait_tok("scalar", stats_tok[c - 2])
                r.emit("scalar", lambda h, c=c, buf=buf: h.copy(buf[:, 0:1024], Xv(c)))
                ct = r.emit("scalar", lambda h, c=c, buf=buf:
                            h.activation(out=buf[:, 1024:2048], in_=Xv(c),
                                         func=AF.Square), inc="sACT")
                r.wait_tok("tensor", ct)
                for th in range(2):
                    r.emit("tensor", lambda h, c=c, th=th, buf=buf:
                           h.matmul(big(th), ONES[:, :],
                                    buf[:, th * 512:(th + 1) * 512],
                                    start=(c == 0), stop=(c == NCH - 1)))
                    r.emit("tensor", lambda h, c=c, th=th, buf=buf:
                           h.matmul(big(2 + th), ONES[:, :],
                                    buf[:, 1024 + th * 512: 1024 + (th + 1) * 512],
                                    start=(c == 0), stop=(c == NCH - 1)),
                           inc="sPE" if th == 1 else None)
                stats_tok[c] = r.tok("sPE")
            for th in range(2):
                r.wait_tok("scalar", stats_tok[NCH - 1])
                r.emit("scalar", lambda h, th=th:
                       h.activation(out=MU[:, :], in_=big(th), func=AF.Copy,
                                    scale=1.0 / D))
                e2t = r.emit("scalar", lambda h, th=th:
                             h.activation(out=XSTG[:, 0:512], in_=big(2 + th),
                                          func=AF.Copy, scale=1.0 / D), inc="sACT")
                r.wait_tok("vector", e2t)
                r.emit("vector", lambda h: h.tensor_tensor(
                    XSTG[:, 512:1024], MU[:, :], MU[:, :], OP.mult))
                vt = r.emit("vector", lambda h:
                            h.tensor_tensor(XSTG[:, 0:512], XSTG[:, 0:512],
                                            XSTG[:, 512:1024], OP.subtract), inc="sDVE")
                r.wait_tok("scalar", vt)
                r.emit("scalar", lambda h:
                       h.activation(out=XSTG[:, 512:1024], in_=XSTG[:, 0:512],
                                    func=AF.Ln, bias=EPS[:, 0:1]))
                sdt = r.emit("scalar", lambda h:
                             h.activation(out=RS[:, :], in_=XSTG[:, 512:1024],
                                          func=AF.Exp, scale=-0.5), inc="sACT")
                r.wait_tok("vector", sdt)
                for c in range(NCH):
                    r.emit("vector", lambda h, c=c, th=th:
                           h.tensor_tensor(XSTG[:, 0:512],
                                           Xv(c, th * 512, (th + 1) * 512),
                                           MU[:, :], OP.subtract))
                    r.emit("vector", lambda h, c=c, th=th:
                           h.tensor_tensor(XNv(c, th * 512, (th + 1) * 512),
                                           XSTG[:, 0:512], RS[:, :], OP.mult),
                           inc="sDVE")
                xn_half[th] = r.tok("sDVE")
                if th == 0:
                    r.wait_tok("scalar", r.tok("sDVE"))
            for q in range(4):
                pfree[("big", q)] = r.tok("sACT")
            return xn_half

        def ffn(xn_half):
            wslot = [0]
            stt_toks = {}
            spa_free = [None]
            load_plan = []
            for _sp in range(NSP):
                for _pb in range(NPB):
                    for _ib in range(NIB):
                        _i0 = _sp * SPW + _pb * PBW + _ib * IBW
                        load_plan.append((wg, _i0))
                        load_plan.append((wu, _i0))
            issued = [0]
            load_toks = {}

            def issue_wgu_upto(n):
                while issued[0] < min(n, len(load_plan)):
                    mat, i0 = load_plan[issued[0]]
                    slot = issued[0] % 2
                    load_toks[issued[0]] = (load_wgu(mat[:, i0:i0 + IBW], slot), slot)
                    issued[0] += 1
            down_read_tok = None
            h_free = None
            su_free = {}
            for sp in range(NSP):
                hsp_evacs = []
                pending = []
                sdl_pre = {}
                for dc in range(2):
                    sdl_pre[dc] = load_wcol(wd[sp * SPW:(sp + 1) * SPW,
                                               dc * 128:(dc + 1) * 128], split=True)

                def emit_sparsify(ts, h_tok):
                    base = (ts - 1) * PBW
                    Hs = H[:, base:base + 2 * PBW]
                    Hr = Hs.rearrange("p (m r) -> p r m", r=4)
                    Ar = SPA[:, :].rearrange("p (m r) -> p r m", r=4)

                    def m(k):
                        return SPM[:, k * 256:(k + 1) * 256]

                    r.wait_tok("scalar", h_tok)
                    r.wait_tok("scalar", spa_free[0])
                    abs_tok = r.emit("scalar", lambda h, Hs=Hs:
                                     h.activation(out=SPA[:, :], in_=Hs,
                                                  func=AF.Abs), inc="sACT")
                    r.wait_tok("vector", abs_tok)
                    r.emit("vector", lambda h, Ar=Ar, m=m:
                           h.tensor_tensor(m(0), Ar[:, 0, :], Ar[:, 1, :], OP.max))
                    r.emit("vector", lambda h, Ar=Ar, m=m:
                           h.tensor_tensor(m(1), Ar[:, 0, :], Ar[:, 1, :], OP.min))
                    r.emit("vector", lambda h, Ar=Ar, m=m:
                           h.tensor_tensor(m(2), Ar[:, 2, :], Ar[:, 3, :], OP.max))
                    r.emit("vector", lambda h, Ar=Ar, m=m:
                           h.tensor_tensor(m(3), Ar[:, 2, :], Ar[:, 3, :], OP.min))
                    r.emit("vector", lambda h, m=m:
                           h.tensor_tensor(m(0), m(0), m(2), OP.min))
                    r.emit("vector", lambda h, m=m:
                           h.tensor_tensor(m(1), m(1), m(3), OP.max))
                    r.emit("vector", lambda h, m=m:
                           h.tensor_tensor(m(2), m(0), m(1), OP.max))
                    sp_tok = None
                    for rr in range(4):
                        r.emit("vector", lambda h, Ar=Ar, rr=rr, m=m:
                               h.tensor_tensor(m(3), Ar[:, rr, :], m(2),
                                               OP.is_ge))
                        sp_tok = r.emit("vector", lambda h, Hr=Hr, rr=rr, m=m:
                                        h.tensor_tensor(Hr[:, rr, :], Hr[:, rr, :],
                                                        m(3), OP.mult),
                                        inc="sDVE" if rr == 3 else None)
                    spa_free[0] = sp_tok
                    return sp_tok

                def emit_transpose(tpb, ts, sp_tok):
                    r.wait_tok("tensor", sp_tok)
                    half = ts % 2
                    pwait("tensor", ("ptr", half))
                    if down_read_tok is not None:
                        r.wait_tok("vector", down_read_tok)
                    PTR = pt["PTR"]
                    ttk = None
                    for k in range(4):
                        ttk = r.emit("tensor", lambda h, ts=ts, k=k, half=half, PTR=PTR:
                                     h.transpose(PTR[:, half * 1024 + k * 128:
                                                     half * 1024 + (k + 1) * 128],
                                                 H[:, ts * PBW + k * 128:
                                                   ts * PBW + (k + 1) * 128],
                                                 ID16[:, :]),
                                     inc="sPE" if k == 3 else None)
                    r.wait_tok("scalar", ttk)
                    c0 = tpb * 4
                    dstv = HSP[:].rearrange("p (c t) -> p c t", c=16)[
                        :, c0:c0 + 4, ts * 128:(ts + 1) * 128]
                    srcv = PTR[:, half * 1024:half * 1024 + 512].rearrange(
                        "p (k t) -> p k t", k=4)
                    ev = r.emit("scalar", lambda h, dstv=dstv, srcv=srcv:
                                h.copy(dstv, srcv), inc="sACT")
                    pfree[("ptr", half)] = ev
                    hsp_evacs.append(ev)

                for pb in range(NPB):
                    for ib in range(NIB):
                        bk = ((sp * NPB) + pb) * NIB + ib
                        issue_wgu_upto(2 * bk + 2)
                        sgate, gslot = load_toks[2 * bk]
                        sup, uslot = load_toks[2 * bk + 1]
                        r.wait(eng="tensor", key=sgate[0], val=sgate[1])
                        silu_toks = {}
                        ibpar = (pb * NIB + ib) % 2
                        SUb = SU if ibpar == 0 else SCB
                        gate_banks = [(("big", 0), big(0)), (("big", 1), big(1)),
                                      (("pc", 0), PC0)]
                        for ts in range(8):
                            r.wait_tok("tensor", xn_half[ts // 4])
                            gkey, gps = gate_banks[ts % 3]
                            pwait("tensor", gkey)
                            for c in range(NCH):
                                r.emit("tensor", lambda h, c=c, ts=ts, gps=gps, gslot=gslot:
                                       h.matmul(gps[:, 0:IBW],
                                                XNv(c, ts * 128, (ts + 1) * 128),
                                                WGU[:, gslot * NCH * IBW + c * IBW:
                                                    gslot * NCH * IBW + (c + 1) * IBW],
                                                start=(c == 0), stop=(c == NCH - 1)),
                                       inc="sPE" if c == NCH - 1 else None)
                            gt = r.tok("sPE")
                            if pending:
                                emit_transpose(*pending.pop(0))
                            r.wait_tok("scalar", gt)
                            r.wait_tok("scalar", su_free.get((ibpar, ts)))
                            stk = r.emit("scalar", lambda h, ts=ts, gps=gps, SUb=SUb:
                                         h.activation(out=SUb[:, ts * IBW:(ts + 1) * IBW],
                                                      in_=gps[:, 0:IBW],
                                                      func=AF.Silu), inc="sACT")
                            silu_toks[ts] = stk
                            pfree[gkey] = stk
                        wgu_slot_tok[gslot] = r.tok("sPE")
                        if ib == 0:
                            h_free = r.tok("sPE")
                        r.wait(eng="tensor", key=sup[0], val=sup[1])
                        last_ib = ib == NIB - 1
                        up_banks = [(("big", 2), big(2)), (("big", 3), big(3)),
                                    (("pc", 1), PC1)]
                        for ts in range(8):
                            ukey, ups = up_banks[ts % 3]
                            pwait("tensor", ukey)
                            for c in range(NCH):
                                r.emit("tensor", lambda h, c=c, ts=ts, ups=ups, uslot=uslot:
                                       h.matmul(ups[:, 0:IBW],
                                                XNv(c, ts * 128, (ts + 1) * 128),
                                                WGU[:, uslot * NCH * IBW + c * IBW:
                                                    uslot * NCH * IBW + (c + 1) * IBW],
                                                start=(c == 0), stop=(c == NCH - 1)),
                                       inc="sPE" if c == NCH - 1 else None)
                            ut = r.tok("sPE")
                            r.wait_tok("vector", ut)
                            r.wait_tok("vector", silu_toks[ts])
                            if h_free is not None:
                                r.wait_tok("vector", h_free)
                            htk = r.emit("vector", lambda h, ts=ts, ib=ib, ups=ups, SUb=SUb:
                                         h.tensor_tensor(
                                             H[:, ts * PBW + ib * IBW:
                                               ts * PBW + (ib + 1) * IBW],
                                             SUb[:, ts * IBW:(ts + 1) * IBW],
                                             ups[:, 0:IBW], OP.mult), inc="sDVE")
                            su_free[(ibpar, ts)] = htk
                            pfree[ukey] = htk
                            if last_ib and ts % 2 == 1:
                                ptok = emit_sparsify(ts, htk)
                                pending.append((pb, ts - 1, ptok))
                                pending.append((pb, ts, ptok))
                        wgu_slot_tok[uslot] = r.tok("sPE")
                    if pb == NPB - 1:
                        while pending:
                            emit_transpose(*pending.pop(0))
                        h_free = r.tok("sPE")
                hsp_done = hsp_evacs[-1]
                for dc in range(NCH):
                    if dc == 8:
                        issue_wgu_upto(2 * ((sp + 1) * NPB * NIB) + 2)
                    if dc < 2:
                        sdl = sdl_pre[dc]
                    else:
                        sdl = load_wcol(wd[sp * SPW:(sp + 1) * SPW,
                                           dc * 128:(dc + 1) * 128], split=True)
                    cslot = sdl[2]
                    r.wait(eng="tensor", key=sdl[0], val=sdl[1])
                    r.wait_tok("tensor", hsp_done)
                    down_banks = [(("pc", 0), PC0), (("pc", 1), PC1),
                                  (("big", 0), big(0)), (("big", 1), big(1))]
                    for th in range(2):
                        dkey, pc = down_banks[(dc * 2 + th) % 4]
                        pwait("tensor", dkey)
                        for c in range(NCH):
                            if c == 8 and sdl[3] is not None:
                                r.wait(eng="tensor", key=sdl[3][0], val=sdl[3][1])
                            r.emit("tensor", lambda h, pc=pc, c=c, th=th, cslot=cslot:
                                   h.matmul(pc[:, :],
                                            WCOL[:, cslot * 2048 + c * 128:
                                                 cslot * 2048 + (c + 1) * 128],
                                            HSPv(c, th * 512, (th + 1) * 512),
                                            start=(c == 0), stop=(c == NCH - 1)),
                                   inc="sPE" if c == NCH - 1 else None)
                        dt = r.tok("sPE")
                        r.wait_tok("vector", dt)
                        stk = r.emit("vector", lambda h, pc=pc, dc=dc, th=th:
                                     h.scalar_tensor_tensor(
                                         Xv(dc, th * 512, (th + 1) * 512), pc[:, :],
                                         0.5, Xv(dc, th * 512, (th + 1) * 512),
                                         OP.mult, OP.add), inc="sDVE")
                        pfree[dkey] = stk
                        stt_toks[(dc, th)] = stk
                    wcol_slot_tok[cslot] = r.tok("sPE")
                down_read_tok = r.tok("sPE")
            x_tok_map = {}
            for c in range(NCH):
                a, b = stt_toks[(c, 0)], stt_toks[(c, 1)]
                x_tok_map[c] = b if b[1] >= a[1] else a
            return x_tok_map

        def attention(xn_half):
            pq_free = {}
            scf_free = None
            pq2_gate = [None]
            qb_gate = {0: None, 1: None}

            def qkv_chunk(oc, dest_kind, dest_slot):
                nonlocal pq_free, scf_free
                sload = load_wcol(wqkv[:, oc * 128:(oc + 1) * 128])
                cslot = sload[2]
                r.wait(eng="tensor", key=sload[0], val=sload[1])
                last = None
                for th in range(2):
                    PQb = pt["PQ"] if (th == 0 or "PQ2" not in pt) else pt["PQ2"]
                    key = "PQ" if PQb is pt["PQ"] else "PQ2"
                    r.wait_tok("tensor", xn_half[th])
                    for tk in pq_free.get(key, []):
                        r.wait_tok("tensor", tk)
                    if key == "PQ2" and pq2_gate[0] is not None:
                        r.wait_tok("tensor", pq2_gate[0])
                        pq2_gate[0] = None
                    for c in range(NCH):
                        r.emit("tensor", lambda h, c=c, th=th, cslot=cslot, PQb=PQb:
                               h.matmul(PQb[:, :],
                                        WCOL[:, cslot * 2048 + c * 128:
                                             cslot * 2048 + (c + 1) * 128],
                                        XNv(c, th * 512, (th + 1) * 512),
                                        start=(c == 0), stop=(c == NCH - 1)),
                               inc="sPE" if c == NCH - 1 else None)
                    qt = r.tok("sPE")
                    if dest_kind == "v":
                        r.wait_tok("scalar", qt)
                        r.wait_tok("scalar", scf_free)
                        vct = r.emit("scalar", lambda h, PQb=PQb:
                                     h.activation(out=SCF[:, 0:512], in_=PQb[:, :],
                                                  func=AF.Copy), inc="sACT")
                        pq_free[key] = [vct]
                        r.wait_tok("tensor", vct)
                        pwait("tensor", ("ptr", 0))
                        vtt = None
                        PTRV = pt["PTRV"]
                        for k in range(4):
                            vtt = r.emit("tensor", lambda h, k=k, PTRV=PTRV:
                                         h.transpose(PTRV[:, k * 128:(k + 1) * 128],
                                                     SCF[:, k * 128:(k + 1) * 128],
                                                     ID16[:, :]),
                                         inc="sPE" if k == 3 else None)
                        r.wait_tok("vector", vtt)
                        last = r.emit("vector", lambda h, dest_slot=dest_slot, th=th:
                                      h.tensor_copy(
                                          SCB[:, dest_slot * 1024 + th * 512:
                                              dest_slot * 1024 + (th + 1) * 512],
                                          pt["PTRV"][:, 0:512]), inc="sDVE")
                        pfree[("ptr", 0)] = last
                        scf_free = last
                    elif "rot" not in ATTN_PARTS:
                        r.wait_tok("scalar", qt)
                        last = r.emit("scalar", lambda h, dest_slot=dest_slot, th=th:
                                      h.activation(out=SCB[:, dest_slot * 1024 + th * 512:
                                                       dest_slot * 1024 + (th + 1) * 512],
                                                   in_=pt["PQ"][:, :], func=AF.Copy),
                                      inc="sACT")
                        pq_free = [last]
                    else:
                        r.wait_tok("scalar", qt)
                        r.wait_tok("scalar", scf_free)
                        r.emit("scalar", lambda h, PQb=PQb:
                               h.copy(SCF[0:64, 512:1024], PQb[64:128, :]))
                        swt = r.emit("scalar", lambda h, PQb=PQb:
                                     h.copy(SCF[64:128, 512:1024], PQb[0:64, :]),
                                     inc="sACT")
                        r.wait_tok("vector", qt)
                        r.wait_tok("vector", swt)
                        r.emit("vector", lambda h, th=th, PQb=PQb:
                               h.tensor_tensor(SCF[:, 0:512], PQb[:, :],
                                               TBA[:, th * 512:(th + 1) * 512],
                                               OP.mult))
                        r.emit("vector", lambda h, th=th:
                               h.tensor_tensor(SCF[:, 512:1024], SCF[:, 512:1024],
                                               TBB[:, th * 512:(th + 1) * 512],
                                               OP.mult))
                        if dest_kind == "k":
                            dv = SCB[:, dest_slot * 1024 + th * 512:
                                     dest_slot * 1024 + (th + 1) * 512]
                        else:
                            r.wait_tok("vector", qb_gate[dest_slot % 2])
                            dv = QB[:, (dest_slot % 2) * 1024 + th * 512:
                                    (dest_slot % 2) * 1024 + (th + 1) * 512]
                        last = r.emit("vector", lambda h, dv=dv:
                                      h.tensor_tensor(dv, SCF[:, 0:512],
                                                      SCF[:, 512:1024], OP.add),
                                      inc="sDVE")
                        pq_free[key] = [last]
                        scf_free = last
                wcol_slot_tok[cslot] = r.tok("sPE")
                return last

            ptrv_stack = contextlib.ExitStack()
            pt["PTRV"] = ptrv_stack.enter_context(
                nc.psum_tensor("PTRVa", [128, 1024], F16))
            kv_send = []
            for kv in range(2):
                if "k%d" % kv not in ATTN_PARTS:
                    continue
                ktok = qkv_chunk(16 + kv, "k", kv)
                if "send" not in ATTN_PARTS:
                    continue
                r.wait_tok("sync", ktok)
                kv_send.append(r.emit("sync", lambda h, kv=kv:
                                      h.dma_start(out=cc_send[kv * 128:(kv + 1) * 128, :],
                                                  in_=SCB[:, kv * 1024:(kv + 1) * 1024]),
                                      inc="sIO", n=16))
            for kv in range(2):
                if "v%d" % kv not in ATTN_PARTS:
                    continue
                if "send" in ATTN_PARTS:
                    r.wait_tok("vector", kv_send[kv])
                vtok = qkv_chunk(18 + kv, "v", kv)
                if "send" not in ATTN_PARTS:
                    continue
                r.wait_tok("sync", vtok)
                kv_send.append(r.emit("sync", lambda h, kv=kv:
                                      h.dma_start(
                                          out=cc_send[256 + kv * 128:
                                                      256 + (kv + 1) * 128, :],
                                          in_=SCB[:, kv * 1024:(kv + 1) * 1024]),
                                      inc="sIO", n=16))
            if "cc" not in ATTN_PARTS:
                return None
            ptrv_stack.close()
            pq2_stack = contextlib.ExitStack()
            pt["PQ2"] = pq2_stack.enter_context(
                nc.psum_tensor("PQ2a", [128, 512], F32))
            pq2_gate[0] = r.tok("sDVE")
            r.wait(eng="gpsimd", key="sIO", val=r.cnt["sIO"])
            r.emit("gpsimd", lambda h:
                   h.collective_compute("AllGather", OP.bypass,
                                        replica_groups=[[0, 1], [2, 3], [4, 5], [6, 7]],
                                        ins=[cc_send.ap().opt()],
                                        outs=[cc_recv.ap().opt()]), inc="sCC")

            def load_kv(kv, gate_tok):
                r.wait(eng="sync", key="sCC", val=1)
                r.wait_tok("sync", gate_tok)
                for dst, row in ((KG[:, 0:1024], kv * 128),
                                 (KG[:, 1024:2048], 512 + kv * 128),
                                 (VT[:, 0:1024], 256 + kv * 128),
                                 (VT[:, 1024:2048], 768 + kv * 128)):
                    r.emit("sync", lambda h, dst=dst, row=row:
                           h.dma_start(out=dst, in_=cc_recv[row:row + 128, :]),
                           inc="sIO", n=16)
                return r.tok("sIO")

            kv_tok = load_kv(0, None)
            if "q" not in ATTN_PARTS:
                return None
            q_toks = {0: qkv_chunk(0, "q", 0), 1: qkv_chunk(1, "q", 1)}
            dn_o_free = None
            a_last = None
            head_pass_end = {}

            if "heads" not in ATTN_PARTS:
                return None
            for hq in range(NH):
                if hq == 8:
                    kv_tok = load_kv(1, head_pass_end[7])
                r.wait(eng="tensor", key=kv_tok[0], val=kv_tok[1])
                r.wait_tok("tensor", q_toks[hq])
                ebf_free = {0: None, 1: None}
                for b in range(2):
                    tiles = []
                    for mm in range(4 * b + 4):
                        for pi in range(2):
                            N = 512 if mm < 4 * b else 512 - 128 * (mm - 4 * b)
                            tiles.append((pi, mm, N))
                    batches = []
                    cur, off = [], 0
                    for (pi, mm, N) in tiles:
                        noff = off
                        if noff % 512 + N > 512:
                            noff = (noff // 512 + 1) * 512
                        if noff + N > 1024:
                            batches.append(cur)
                            cur, noff = [], 0
                        cur.append((pi, mm, N, noff))
                        off = noff + N
                    if cur:
                        batches.append(cur)
                    first = [True]
                    pend = []

                    def emit_dnpv(batch, eslot, ext, mask_tok, last_batch):
                        r.wait_tok("tensor", ext)
                        if mask_tok is not None:
                            r.wait_tok("tensor", mask_tok)
                        if first[0]:
                            pwait("tensor", ("pc", 0), ("pc", 1))
                            if dn_o_free is not None:
                                r.wait_tok("tensor", dn_o_free)
                        pvt = None
                        for j, (pi, mm, N, off) in enumerate(batch):
                            is_first = first[0] and j == 0
                            is_last = last_batch and j == len(batch) - 1
                            r.emit("tensor",
                                   lambda h, N=N, off=off, eslot=eslot,
                                   is_first=is_first, is_last=is_last:
                                   h.matmul(PC0[:, 512 - N:512], ONES[:, :],
                                            EBF[:, eslot * 1024 + off:
                                                eslot * 1024 + off + N],
                                            start=is_first, stop=is_last))
                            pvt = r.emit("tensor",
                                         lambda h, pi=pi, mm=mm, N=N, off=off,
                                         eslot=eslot, is_first=is_first,
                                         is_last=is_last:
                                         h.matmul(PC1[:, 512 - N:512],
                                                  VT[:, (pi * 8 + mm) * 128:
                                                     (pi * 8 + mm + 1) * 128],
                                                  EBF[:, eslot * 1024 + off:
                                                      eslot * 1024 + off + N],
                                                  start=is_first, stop=is_last),
                                         inc="sPE" if j == len(batch) - 1 else None)
                        ebf_free[eslot] = pvt
                        first[0] = False

                    for bi, batch in enumerate(batches):
                        sb = bi % 2
                        eslot = bi % 2
                        pwait("tensor", ("big", 2 * sb), ("big", 2 * sb + 1))
                        W = max(off2 + N2 for (pi2, mm2, N2, off2) in batch)
                        spt = None
                        for j, (pi, mm, N, off) in enumerate(batch):
                            qlo = b * 512 + (512 - N)
                            spt = r.emit(
                                "tensor",
                                lambda h, pi=pi, mm=mm, N=N, off=off, sb=sb, hq=hq,
                                qlo=qlo:
                                h.matmul(bighalf(sb)[:, off:off + N],
                                         KG[:, pi * 1024 + mm * 128:
                                            pi * 1024 + (mm + 1) * 128],
                                         QB[:, (hq % 2) * 1024 + qlo:
                                            (hq % 2) * 1024 + qlo + N],
                                         start=True, stop=True),
                                inc="sPE" if j == len(batch) - 1 else None)
                        r.wait_tok("scalar", spt)
                        r.wait_tok("scalar", ebf_free[eslot])
                        r.emit("scalar", lambda h, sb=sb, W=W:
                               h.activation(out=ESB[:, 0:W], in_=bighalf(sb)[:, 0:W],
                                            func=AF.Tanh, scale=SCALE / CAP))
                        ext = r.emit("scalar", lambda h, eslot=eslot, W=W:
                                     h.activation(
                                         out=EBF[:, eslot * 1024:eslot * 1024 + W],
                                         in_=ESB[:, 0:W], func=AF.Exp, scale=CAP),
                                     inc="sACT")
                        pfree[("big", 2 * sb)] = ext
                        pfree[("big", 2 * sb + 1)] = ext
                        mask_tok = None
                        diag = [t for t in batch if t[1] >= 4 * b]
                        if diag:
                            r.wait_tok("vector", ext)
                            for j, (pi, mm, N, off) in enumerate(diag):
                                mask_tok = r.emit(
                                    "vector",
                                    lambda h, eslot=eslot, off=off, pi=pi:
                                    h.tensor_tensor(
                                        EBF[:, eslot * 1024 + off:
                                            eslot * 1024 + off + 128],
                                        EBF[:, eslot * 1024 + off:
                                            eslot * 1024 + off + 128],
                                        (MP0 if pi == 0 else MP1)[:, :], OP.mult),
                                    inc="sDVE" if j == len(diag) - 1 else None)
                        pend.append((batch, eslot, ext, mask_tok,
                                     bi == len(batches) - 1))
                        if len(pend) > 1:
                            emit_dnpv(*pend.pop(0))
                    while pend:
                        emit_dnpv(*pend.pop(0))
                    r.wait_tok("vector", r.tok("sPE"))
                    r.emit("vector", lambda h: h.reciprocal(XSTG[:, 0:512], PC0[:, :]))
                    aev = r.emit("vector", lambda h, hq=hq, b=b:
                                 h.tensor_tensor(HSPv(hq, b * 512, (b + 1) * 512),
                                                 PC1[:, :], XSTG[:, 0:512], OP.mult),
                                 inc="sDVE")
                    dn_o_free = aev
                    a_last = aev
                head_pass_end[hq] = r.tok("sPE")
                qb_gate[hq % 2] = head_pass_end[hq]
                if hq + 2 < NH:
                    q_toks[hq + 2] = qkv_chunk(hq + 2, "q", hq + 2)
            if "wo" not in ATTN_PARTS:
                return None
            stt_toks = {}
            for dc in range(NCH):
                sload = load_wcol(wo[:, dc * 128:(dc + 1) * 128])
                cslot = sload[2]
                r.wait(eng="tensor", key=sload[0], val=sload[1])
                r.wait_tok("tensor", a_last)
                if dn_o_free is not None:
                    r.wait_tok("tensor", dn_o_free)
                wo_banks = [(("pc", 0), PC0), (("pc", 1), PC1),
                            (("big", 0), big(0)), (("big", 1), big(1))]
                for th in range(2):
                    wkey, pc = wo_banks[(dc * 2 + th) % 4]
                    pwait("tensor", wkey)
                    for c in range(NCH):
                        r.emit("tensor", lambda h, pc=pc, c=c, th=th, cslot=cslot:
                               h.matmul(pc[:, :],
                                        WCOL[:, cslot * 2048 + c * 128:
                                             cslot * 2048 + (c + 1) * 128],
                                        HSPv(c, th * 512, (th + 1) * 512),
                                        start=(c == 0), stop=(c == NCH - 1)),
                               inc="sPE" if c == NCH - 1 else None)
                    wt = r.tok("sPE")
                    r.wait_tok("vector", wt)
                    stk = r.emit("vector", lambda h, pc=pc, dc=dc, th=th:
                                 h.scalar_tensor_tensor(
                                     Xv(dc, th * 512, (th + 1) * 512), pc[:, :], 1.0,
                                     Xv(dc, th * 512, (th + 1) * 512),
                                     OP.mult, OP.add), inc="sDVE")
                    pfree[wkey] = stk
                    stt_toks[(dc, th)] = stk
                wcol_slot_tok[cslot] = r.tok("sPE")
            x_tok_map = {}
            for c in range(NCH):
                a, b = stt_toks[(c, 0)], stt_toks[(c, 1)]
                x_tok_map[c] = b if b[1] >= a[1] else a
            pq2_stack.close()
            pt.pop("PQ2", None)
            return x_tok_map

        x_map = x_ready
        if "ffn1" in PHASES:
            xa = layernorm(x_map)
            with nc.psum_tensor("PTRa", [128, 2048], F16) as _p:
                pt["PTR"] = _p
                x_map = ffn(xa)
        if "attn" in PHASES:
            xa = layernorm(x_map)
            with nc.psum_tensor("PQa", [128, 512], F32) as _pq:
                pt["PQ"] = _pq
                xm2 = attention(xa)
            if xm2 is not None:
                x_map = xm2
        if "ffn2" in PHASES:
            xa = layernorm(x_map)
            with nc.psum_tensor("PTRb", [128, 2048], F16) as _p:
                pt["PTR"] = _p
                x_map = ffn(xa)

        stg_free = {0: None, 1: None}
        for ts in range(8):
            for dh in range(2):
                for k4 in range(2):
                    half = (ts * 4 + dh * 2 + k4) % 2
                    q = k4
                    pwait("tensor", ("big", q))
                    tk = None
                    for k in range(4):
                        c = dh * 8 + k4 * 4 + k
                        r.wait_tok("tensor", x_map.get(c))
                        tk = r.emit("tensor", lambda h, c=c, ts=ts, q=q, k=k:
                                    h.transpose(big(q)[:, k * 128:(k + 1) * 128],
                                                Xv(c, ts * 128, (ts + 1) * 128),
                                                ID32[:, :]),
                                    inc="sPE" if k == 3 else None)
                    r.wait_tok("vector", tk)
                    r.wait_tok("vector", stg_free[half])
                    ev = r.emit("vector", lambda h, half=half, q=q:
                                h.tensor_copy(XSTG[:, half * 512:(half + 1) * 512],
                                              big(q)), inc="sDVE")
                    pfree[("big", q)] = ev
                    r.wait_tok("sync", ev)
                    dt = r.emit("sync", lambda h, ts=ts, dh=dh, k4=k4, half=half:
                                h.dma_start(out=out[ts * 128:(ts + 1) * 128,
                                                    dh * 1024 + k4 * 512:
                                                    dh * 1024 + (k4 + 1) * 512],
                                            in_=XSTG[:, half * 512:(half + 1) * 512]),
                                inc="sIO", n=16)
                    stg_free[half] = dt
        r.wait(eng="sync", key="sIO", val=r.cnt["sIO"])

        def make_body(ename):
            ops = r.ops[ename]

            def body(h):
                for op in ops:
                    if op[0] == "w":
                        h.wait_ge(sems[op[1]], op[2])
                    else:
                        inst = op[1](h)
                        if op[2] is not None:
                            inst.then_inc(sems[op[2]], op[3])
            return body

        blk.tensor(make_body("tensor"))
        blk.vector(make_body("vector"))
        blk.scalar(make_body("scalar"))
        blk.gpsimd(make_body("gpsimd"))
        blk.sync(make_body("sync"))

    return nc



def _host_tables(parity):
    inv_freq = 1.0 / (10000.0 ** (np.arange(0, HD, 2, dtype=np.float64) / HD))
    pos = 2.0 * np.arange(NT, dtype=np.float64) + parity
    fr = np.outer(inv_freq, pos)
    c = np.cos(fr)
    s = np.sin(fr)
    tbl_a = np.concatenate([c, c], axis=0).astype(np.float16)
    tbl_b = np.concatenate([-s, s], axis=0).astype(np.float16)
    return tbl_a, tbl_b


def kernel(L0_attn_hidden_states, L0_attn_Wqkv, L0_attn_Wo,
           L0_ffn_w_gate, L0_ffn_w_up, L0_ffn_w_down):
    x = np.asarray(L0_attn_hidden_states)
    wqkv = np.ascontiguousarray(np.asarray(L0_attn_Wqkv), dtype=np.float32)
    wov = np.ascontiguousarray(np.asarray(L0_attn_Wo), dtype=np.float32)
    wgt = np.ascontiguousarray(np.asarray(L0_ffn_w_gate), dtype=np.float32)
    wup = np.ascontiguousarray(np.asarray(L0_ffn_w_up), dtype=np.float32)
    wdn = np.ascontiguousarray(np.asarray(L0_ffn_w_down), dtype=np.float32)

    jj = np.arange(128)[:, None]
    ii = np.arange(128)[None, :]
    mp0 = (jj <= ii).astype(ml_dtypes.bfloat16)
    mp1_even = (jj < ii).astype(ml_dtypes.bfloat16)
    mp1_odd = (jj <= ii).astype(ml_dtypes.bfloat16)
    id32 = np.eye(128, dtype=np.float32)
    id16 = np.eye(128, dtype=np.float16)
    ones_bf = np.ones((128, 128), dtype=ml_dtypes.bfloat16)

    nc = build_program()
    in_maps = []
    for core in range(8):
        b = core // 2
        parity = core % 2
        tbl_a, tbl_b = _host_tables(parity)
        in_maps.append({
            "x": np.ascontiguousarray(x[b, parity::2, :], dtype=np.float32),
            "wqkv": wqkv, "wo": wov, "wg": wgt, "wu": wup, "wd": wdn,
            "tbl_a": tbl_a, "tbl_b": tbl_b,
            "mask_p0": mp0, "mask_p1": (mp1_even if parity == 0 else mp1_odd),
            "ident32": id32, "ident16": id16, "ones_bf": ones_bf,
            "eps32": np.full((128, 1), LN_EPS, dtype=np.float32),
        })
    res = run_bass_kernel_spmd(nc, in_maps, core_ids=list(range(8)))
    outp = np.empty((4, 2048, 2048), dtype=np.float32)
    for core in range(8):
        b = core // 2
        parity = core % 2
        outp[b, parity::2, :] = res.results[core]["out"]
    return outp
